# version 08
# baseline (speedup 1.0000x reference)
from __future__ import annotations
from concourse.bass_utils import run_bass_kernel_spmd
"""Mini raw-Bass emission framework with automatic semaphore insertion.

Two-phase: record ops (each = exactly one hardware instruction) with explicit
read/write regions, then materialize into per-engine instruction streams with
semaphore waits derived from a vector-clock dependency analysis.

Engines: 'pe' (TensorE), 'act' (ScalarE), 'dve' (VectorE), 'pool' (GpSimd),
'sp' (Sync).  DMA ops are attributed to the issuing engine ('sp'/'act'/'pool')
and complete on that queue's DMA semaphore (+16 per dma_start).
"""

import contextlib

import concourse.bass as bass
import concourse.mybir as mybir

F32 = mybir.dt.float32
BF16 = mybir.dt.bfloat16
AF = mybir.ActivationFunctionType
ALU = mybir.AluOpType
AX = mybir.AxisListType

ENGINES = ("pe", "act", "dve", "pool", "sp")
DMA_QUEUES = {"sp": "q_sp", "act": "q_act", "pool": "q_pool"}
ALL_SEMS = list(ENGINES) + list(DMA_QUEUES.values())
SEM_IDX = {name: i for i, name in enumerate(ALL_SEMS)}
ENG_IDX = {name: i for i, name in enumerate(ENGINES)}

SAME_ENGINE_SAFE = {"pe": True, "act": False, "dve": False, "pool": False, "sp": True}

FULL_SERIAL = True


class TV:

    __slots__ = ("h", "name", "space", "shape", "dtype", "p0", "p1", "f0", "f1")

    def __init__(self, h, name, space, shape, dtype, p0, p1, f0, f1):
        self.h = h
        self.name = name
        self.space = space
        self.shape = shape
        self.dtype = dtype
        self.p0, self.p1 = p0, p1
        self.f0, self.f1 = f0, f1

    def __getitem__(self, idx):
        assert self.space in ("sbuf", "psum")
        ps, fs = idx
        p0, p1 = _norm_slice(ps, self.p0, self.p1)
        f0, f1 = _norm_slice(fs, self.f0, self.f1)
        return TV(self.h, self.name, self.space, self.shape, self.dtype, p0, p1, f0, f1)

    def flat(self, e0, e1):
        assert self.space == "dram"
        assert self.p0 <= e0 <= e1 <= self.p1, (self.name, self.p0, self.p1, e0, e1)
        return TV(self.h, self.name, self.space, self.shape, self.dtype, e0, e1, 0, 0)

    def _flat_ap(self):
        a = self.h.ap()
        if len(self.shape) != 1:
            a = a.flatten()
        return a

    def ap(self):
        if self.space in ("sbuf", "psum"):
            a = self.h.ap()
            if a.dtype != self.dtype:
                a = a.bitcast(self.dtype)
            return a[self.p0:self.p1, self.f0:self.f1]
        return self._flat_ap()[self.p0:self.p1]

    def ap2d(self, rows, cols, row_pitch=None):
        assert self.space == "dram"
        pitch = cols if row_pitch is None else row_pitch
        a = self._flat_ap()
        import dataclasses
        return dataclasses.replace(a, offset=a.offset + self.p0,
                                   ap=[[pitch, rows], [1, cols]])

    def nd(self, *slices):
        assert self.space == "dram"
        ap = self.h.ap()[tuple(slices)]
        strides = []
        acc = 1
        for s in reversed(self.shape):
            strides.append(acc)
            acc *= s
        strides = strides[::-1]
        lo = 0
        hi = 0
        for i, dim in enumerate(self.shape):
            s = slices[i] if i < len(slices) else slice(None)
            if isinstance(s, int):
                a0, b0 = s, s + 1
            elif isinstance(s, slice):
                a0 = 0 if s.start is None else s.start
                b0 = dim if s.stop is None else s.stop
            else:
                raise TypeError(s)
            lo += a0 * strides[i]
            hi += (b0 - 1) * strides[i]
        return ap, (self.name, 0, 1, lo, hi + 1)

    def region(self):
        if self.space == "dram":
            return (self.name, 0, 1, self.p0, self.p1)
        b0 = self.f0 * _dtb(self.dtype)
        b1 = self.f1 * _dtb(self.dtype)
        if self.space == "psum":
            b0 = (b0 // 2048) * 2048
            b1 = ((b1 + 2047) // 2048) * 2048
        return (self.name, self.p0, self.p1, b0, b1)

    @property
    def pn(self):
        return self.p1 - self.p0

    @property
    def fn(self):
        return self.f1 - self.f0


def _norm_slice(s, lo, hi):
    if s is None or s == slice(None):
        return lo, hi
    if isinstance(s, slice):
        a = lo if s.start is None else lo + s.start
        b = hi if s.stop is None else lo + s.stop
        assert lo <= a <= b <= hi, (s, lo, hi)
        return a, b
    raise TypeError(s)


def _dtb(dt):
    return {F32: 4, BF16: 2, mybir.dt.float16: 2, mybir.dt.int32: 4,
            mybir.dt.uint8: 1, mybir.dt.uint16: 2, mybir.dt.int8: 1}[dt]


def regions_overlap(a, b):
    if a[0] != b[0]:
        return False
    if a[1] >= b[2] or b[1] >= a[2]:
        return False
    if a[3] >= b[4] or b[3] >= a[4]:
        return False
    return True


class Op:
    __slots__ = ("engine", "fn", "reads", "writes", "is_dma", "seq", "dseq",
                 "idx", "needs_inc", "deps", "tag")

    def __init__(self, engine, fn, reads, writes, is_dma, tag=""):
        self.engine = engine
        self.fn = fn
        self.reads = reads
        self.writes = writes
        self.is_dma = is_dma
        self.seq = 0
        self.dseq = 0
        self.idx = 0
        self.needs_inc = False
        self.deps = []
        self.tag = tag


class Prog:
    def __init__(self, nc: bass.Bass):
        self.nc = nc
        self.ops: list[Op] = []
        self.eng_count = {e: 0 for e in ENGINES}
        self.q_count = {q: 0 for q in DMA_QUEUES.values()}
        self._tensors = {}
        self._consts = {}
        self._dram_names = set()
        self.ctx = contextlib.ExitStack()
        self._sbuf_bytes = 0
        self.trace_tags = False

    def sbuf(self, name, p, f, dtype):
        h = self.ctx.enter_context(self.nc.sbuf_tensor(name, [p, f], dtype))
        self._sbuf_bytes += f * _dtb(dtype)
        return TV(h, name, "sbuf", (p, f), dtype, 0, p, 0, f)

    def psum(self, name, p, f, dtype=F32):
        h = self.ctx.enter_context(self.nc.psum_tensor(name, [p, f], dtype))
        return TV(h, name, "psum", (p, f), dtype, 0, p, 0, f)

    def dram(self, name, shape, dtype, kind="Internal"):
        h = self.nc.dram_tensor(name, list(shape), dtype, kind=kind)
        flatlen = 1
        for s in shape:
            flatlen *= s
        assert name not in self._dram_names
        self._dram_names.add(name)
        return TV(h, name, "dram", tuple(shape), dtype, 0, flatlen, 0, 0)

    def free(self, tv):
        pass

    @staticmethod
    def carve(arena, p0, p1, f0, f1, dtype=None):
        dt = arena.dtype if dtype is None else dtype
        return TV(arena.h, arena.name, arena.space, arena.shape, dt, p0, p1, f0, f1)

    def add(self, engine, fn, reads=(), writes=(), dma=False, tag=""):
        assert engine in ENGINES
        if dma:
            assert engine in DMA_QUEUES, f"engine {engine} cannot issue DMA"
        op = Op(engine, fn, [r.region() if isinstance(r, TV) else r for r in reads],
                [w.region() if isinstance(w, TV) else w for w in writes], dma, tag=tag)
        op.idx = len(self.ops)
        self.eng_count[engine] += 1
        op.seq = self.eng_count[engine]
        if dma:
            q = DMA_QUEUES[engine]
            self.q_count[q] += 1
            op.dseq = self.q_count[q]
        self.ops.append(op)
        return op

    def dma(self, queue, out, in_, tag="", **kw):
        oap, oreg = _apreg(out)
        iap, ireg = _apreg(in_)
        return self.add(queue, lambda e: e.dma_start(out=oap, in_=iap, **kw),
                        reads=[ireg], writes=[oreg], dma=True, tag=tag)

    def dma_transpose(self, queue, out, in_, tag=""):
        oap, oreg = _apreg(out)
        iap, ireg = _apreg(in_)
        return self.add(queue, lambda e: e.dma_start_transpose(out=oap, in_=iap),
                        reads=[ireg], writes=[oreg], dma=True, tag=tag)

    def mm(self, out, lhsT, rhs, start=True, stop=True, tag="", **kw):
        if isinstance(out, TV) and out.space == "psum":
            b0 = out.f0 * _dtb(out.dtype)
            b1 = out.f1 * _dtb(out.dtype) - 1
            assert b0 // 2048 == b1 // 2048, f"matmul output straddles PSUM banks: {out.name} {b0}..{b1}"
        oap, oreg = _apreg(out)
        lap, lreg = _apreg(lhsT)
        rap, rreg = _apreg(rhs)
        return self.add("pe", lambda e: e.matmul(oap, lap, rap, start=start, stop=stop, **kw),
                        reads=[lreg, rreg], writes=[oreg], tag=tag)

    def transpose(self, out, in_, ident, tag=""):
        oap, oreg = _apreg(out)
        iap, ireg = _apreg(in_)
        idap, idreg = _apreg(ident)
        return self.add("pe", lambda e: e.transpose(oap, iap, idap),
                        reads=[ireg, idreg], writes=[oreg], tag=tag)

    def const(self, value, dtype=F32):
        key = (float(value), dtype)
        if key not in self._consts:
            tv = self.sbuf(f"konst{len(self._consts)}", 128, 1, dtype)
            self.nc.const_aps.register(dtype, float(value), tv.ap())
            self.memset(tv, float(value), tag="konst")
            self._consts[key] = tv
        return self._consts[key]

    def act(self, out, in_, func, bias=0.0, scale=1.0, accum_out=None, tag=""):
        if func == AF.Copy and isinstance(bias, TV):
            func = AF.Identity
        oap, oreg = _apreg(out)
        iap, ireg = _apreg(in_)
        reads = [ireg]
        writes = [oreg]
        bias_ap, scale_ap = bias, scale
        if isinstance(bias, TV):
            bias_ap = bias.ap()
            reads.append(bias.region())
        elif isinstance(bias, float) and bias not in (0.0, 1.0):
            reads.append(self.const(bias).region())
        if isinstance(scale, TV):
            scale_ap = scale.ap()
            reads.append(scale.region())
        elif isinstance(scale, float) and scale not in (0.0, 1.0):
            reads.append(self.const(scale).region())
        acc_ap = None
        if accum_out is not None:
            acc_ap = accum_out.ap()
            writes.append(accum_out.region())
        return self.add("act",
                        lambda e: e.activation(oap, iap, func, bias=bias_ap,
                                               scale=scale_ap, accum_out=acc_ap),
                        reads=reads, writes=writes, tag=tag)

    def _veng(self, out, in_list, scalars, fn, tag, eng="dve"):
        oap, oreg = _apreg(out)
        reads = []
        aps = []
        for x in in_list:
            ap, reg = _apreg(x)
            aps.append(ap)
            reads.append(reg)
        sc = []
        for s in scalars:
            if isinstance(s, TV):
                sc.append(s.ap())
                reads.append(s.region())
            else:
                sc.append(s)
        return self.add(eng, lambda e: fn(e, oap, aps, sc), reads=reads,
                        writes=[oreg], tag=tag)

    def tt(self, out, a, b, op, tag=""):
        return self._veng(out, [a, b], [],
                          lambda e, o, i, s: e.tensor_tensor(o, i[0], i[1], op), tag)

    def ts(self, out, a, s1, s2=None, op0=ALU.mult, op1=ALU.bypass, tag=""):
        return self._veng(out, [a], [s1] + ([] if s2 is None else [s2]),
                          lambda e, o, i, s: e.tensor_scalar(
                              o, i[0], s[0], (s[1] if len(s) > 1 else None), op0, op1),
                          tag)

    def stt(self, out, in0, scalar, in1, op0, op1, tag=""):
        return self._veng(out, [in0, in1], [scalar],
                          lambda e, o, i, s: e.scalar_tensor_tensor(
                              o, i[0], s[0], i[1], op0, op1), tag)

    def scan(self, out, data0, data1, initial, op0, op1, tag=""):
        return self._veng(out, [data0, data1], [initial],
                          lambda e, o, i, s: e.tensor_tensor_scan(
                              o, i[0], i[1], s[0], op0, op1), tag)

    def reduce(self, out, in_, op=ALU.max, negate=None, tag=""):
        return self._veng(out, [in_], [],
                          lambda e, o, i, s: e.tensor_reduce(
                              o, i[0], AX.X, op, negate=negate), tag)

    def copy(self, out, in_, eng="dve", tag=""):
        return self._veng(out, [in_], [],
                          lambda e, o, i, s: e.tensor_copy(o, i[0]), tag, eng=eng)

    def recip(self, out, in_, tag=""):
        return self._veng(out, [in_], [],
                          lambda e, o, i, s: e.reciprocal(o, i[0]), tag)

    def memset(self, tv, value, eng="dve", tag=""):
        ap, reg = _apreg(tv)
        return self.add(eng, lambda e: e.memset(ap, value), reads=[], writes=[reg], tag=tag)

    def amemzero(self, tv, tag=""):
        ap, reg = _apreg(tv)
        return self.add("act", lambda e: e.memzero(ap), reads=[], writes=[reg], tag=tag)

    def afill(self, tv, value, tag=""):
        self.amemzero(tv, tag=f"{tag}z")
        return self.act(tv, tv, AF.Copy, bias=float(value), tag=f"{tag}f")

    def iota(self, tv, pattern, base=0, channel_multiplier=0, tag=""):
        ap, reg = _apreg(tv)
        return self.add("pool",
                        lambda e: e.iota(ap, pattern, base=base,
                                         channel_multiplier=channel_multiplier),
                        reads=[], writes=[reg], tag=tag)

    def affine_select(self, out, in_, pattern, compare_op, fill, base=0,
                      channel_multiplier=0, tag=""):
        oap, oreg = _apreg(out)
        iap, ireg = _apreg(in_)
        return self.add("pool",
                        lambda e: e.affine_select(oap, iap, pattern, compare_op, fill,
                                                  base=base,
                                                  channel_multiplier=channel_multiplier),
                        reads=[ireg], writes=[oreg], tag=tag)

    def select(self, out, mask, on_true, on_false, tag=""):
        return self._veng(out, [mask, on_true, on_false], [],
                          lambda e, o, i, s: e.select(o, i[0], i[1], i[2]), tag)

    def raw(self, engine, fn, reads=(), writes=(), tag=""):
        rd = [r.region() if isinstance(r, TV) else r for r in reads]
        wr = [w.region() if isinstance(w, TV) else w for w in writes]
        return self.add(engine, fn, rd, wr, tag=tag)

    def _resolve(self):
        writers: dict[str, list] = {}
        readers: dict[str, list] = {}
        nsem = len(ALL_SEMS)
        vc = {e: [0] * nsem for e in ENGINES}
        snapshots = [None] * len(self.ops)

        def comp_identity(op):
            if op.is_dma:
                return SEM_IDX[DMA_QUEUES[op.engine]], op.dseq
            return SEM_IDX[op.engine], op.seq

        for op in self.ops:
            deps = {}
            def consider(d):
                if d is op:
                    return
                sid, sseq = comp_identity(d)
                if d.engine == op.engine:
                    if (not d.is_dma) and (not op.is_dma) and SAME_ENGINE_SAFE[op.engine]:
                        return
                cur = deps.get(sid)
                if cur is None or sseq > cur[0]:
                    deps[sid] = (sseq, d)

            if FULL_SERIAL and op.idx > 0:
                consider(self.ops[op.idx - 1])
            for r in op.reads:
                for (reg, d) in writers.get(r[0], ()):
                    if regions_overlap(r, reg):
                        consider(d)
            for w in op.writes:
                for (reg, d) in writers.get(w[0], ()):
                    if regions_overlap(w, reg):
                        consider(d)
                for (reg, d) in readers.get(w[0], ()):
                    if regions_overlap(w, reg):
                        consider(d)

            myvc = vc[op.engine]
            needed = []
            for sid, (sseq, d) in sorted(deps.items()):
                if myvc[sid] >= sseq:
                    continue
                needed.append((sid, d))
                d.needs_inc = True
                dsnap = snapshots[d.idx]
                if dsnap is not None:
                    for k in range(nsem):
                        if dsnap[k] > myvc[k]:
                            myvc[k] = dsnap[k]
                if sseq > myvc[sid]:
                    myvc[sid] = sseq
            op.deps = needed

            for r in op.reads:
                readers.setdefault(r[0], []).append((r, op))
                self._prune(readers[r[0]])
            for w in op.writes:
                writers.setdefault(w[0], []).append((w, op))
                self._prune(writers[w[0]])

            sid_own, sseq_own = comp_identity(op)
            snap = list(vc[op.engine])
            if sseq_own > snap[sid_own]:
                snap[sid_own] = sseq_own
            snapshots[op.idx] = snap

    @staticmethod
    def _prune(lst, cap=96):
        if len(lst) > cap:
            keep = lst[-cap // 2:]
            old = lst[:-cap // 2]
            name = old[0][0][0]
            by_identity = {}
            for reg, o in old:
                key = (o.engine, o.is_dma)
                cur = by_identity.get(key)
                if cur is None or o.idx > cur[1].idx:
                    by_identity[key] = (reg, o)
            union = (name, 0, 1 << 30, 0, 1 << 40)
            merged = [(union, o) for (_, o) in by_identity.values()]
            lst[:] = merged + keep

    def materialize(self, final_wait_engine="sp"):
        self._resolve()
        nc = self.nc
        inc_count = {}
        run = {e: 0 for e in ENGINES}
        for op in self.ops:
            if op.is_dma:
                op.needs_inc = True
        for op in self.ops:
            if op.is_dma:
                q = DMA_QUEUES[op.engine]
                inc_count[(SEM_IDX[q], op.dseq)] = op.dseq * 16
            elif op.needs_inc:
                run[op.engine] += 1
                inc_count[(SEM_IDX[op.engine], op.seq)] = run[op.engine]
        sems = {}
        per_engine_ops = {e: [] for e in ENGINES}
        for op in self.ops:
            per_engine_ops[op.engine].append(op)
        used = [e for e in ENGINES if per_engine_ops[e]]
        q_final = {q: self.q_count[q] * 16 for q in DMA_QUEUES.values() if self.q_count[q]}
        eng_final = {}
        for e in used:
            last = None
            for op in reversed(per_engine_ops[e]):
                if not op.is_dma:
                    last = op
                    break
            if last is not None and not last.needs_inc:
                last.needs_inc = True
                run[e] += 1
                inc_count[(SEM_IDX[e], last.seq)] = run[e]
            eng_final[e] = run[e]

        with contextlib.ExitStack() as stack:
            for s in ALL_SEMS:
                sems[s] = stack.enter_context(nc.semaphore(f"sem_{s}"))
            block = stack.enter_context(nc.Block())

            def emit_stream(eng_obj, eng_name):
                for op in per_engine_ops[eng_name]:
                    for sid, d in op.deps:
                        if d.is_dma:
                            val = d.dseq * 16
                        else:
                            val = inc_count[(sid, d.seq)]
                        eng_obj.wait_ge(sems[ALL_SEMS[sid]], val)
                    inst = op.fn(eng_obj)
                    if op.is_dma:
                        inst.then_inc(sems[DMA_QUEUES[op.engine]], 16)
                    elif op.needs_inc:
                        inst.then_inc(sems[op.engine], 1)
                if eng_name == final_wait_engine:
                    for q, v in q_final.items():
                        eng_obj.wait_ge(sems[q], v)
                    for e2 in used:
                        if e2 != eng_name and eng_final.get(e2, 0) > 0:
                            eng_obj.wait_ge(sems[e2], eng_final[e2])
                    for s2 in ALL_SEMS:
                        eng_obj.sem_clear(sems[s2])

            attr = {"pe": "tensor", "act": "scalar", "dve": "vector",
                    "pool": "gpsimd", "sp": "sync"}
            if final_wait_engine not in used:
                used.append(final_wait_engine)
            for e in used:
                getattr(block, attr[e])(
                    lambda eng_obj, _e=e: emit_stream(eng_obj, _e))
        self.ctx.close()
        return nc

    def stats(self):
        n_inc = sum(1 for op in self.ops if op.needs_inc)
        n_wait = sum(len(op.deps) for op in self.ops)
        per = {e: self.eng_count[e] for e in ENGINES if self.eng_count[e]}
        return dict(n_ops=len(self.ops), n_inc=n_inc, n_wait=n_wait, per_engine=per,
                    dma={q: n for q, n in self.q_count.items() if n})


def _apreg(x):
    if isinstance(x, TV):
        return x.ap(), x.region()
    if isinstance(x, tuple) and len(x) == 2:
        return x
    raise TypeError(x)


"""Block emitters for the mk_sandwich_d2 megakernel.

Per core: 2 batches x 2048 tokens x 1024 d. Global token g in [0,4096).

DRAM (f32 unless noted):
  R     flat 4096*1024, token-major.
  Xn    bf16 flat 8*128*4096, d-major: [k][p][g], d = 128k+p.
  stats flat 32*128*2 ([negmean, invstd] per token).
  BO    flat 4096*1024 (block raw output).
  F     flat 2048*FPITCH + 2048 (skew scratch).

Engines can't move data across partitions; all evacuations are lane-aligned.
"""

NEG = -1.0e8
FPITCH = 390
D = 1024
T = 2048
NB = 2
G = NB * T
NT = G // 128
EPS = 1e-5


def t_slices(total, step):
    return [(i, min(i + step, total)) for i in range(0, total, step)]


class Ctx:
    def __init__(self, P: Prog):
        self.P = P
        self.A = P.sbuf("A", 128, 86000, BF16)
        self.PSA = P.psum("PSA", 128, 1024, F32)
        self.PSB = P.psum("PSB", 128, 1024, F32)
        self.PSC = P.psum("PSC", 128, 1024, F32)
        self.PSD = P.psum("PSD", 128, 512, F32)
        self.PSE = P.psum("PSE", 128, 512, F32)
        self.ident = P.sbuf("ident", 128, 128, BF16)
        self.ident32 = P.sbuf("ident32", 128, 128, F32)
        self.ones_row = P.sbuf("ones_row", 1, 512, F32)
        self.cols = P.sbuf("cols", 128, 512, F32)
        self.col_n = 0

    def setup(self):
        P = self.P
        tmp16 = self.c16(0, 128)
        tmp16b = self.c16(256, 128)
        P.afill(tmp16, 1.0, tag="id_a")
        P.affine_select(tmp16b, tmp16, pattern=[[-1, 128]],
                        compare_op=ALU.is_equal, fill=0.0, base=0,
                        channel_multiplier=1, tag="ident")
        P.act(self.ident, tmp16b, AF.Copy, tag="id_c")
        tmp32 = self.c32(512, 128)
        tmp32b = self.c32(1024, 128)
        P.afill(tmp32, 1.0, tag="id32_a")
        P.affine_select(tmp32b, tmp32, pattern=[[-1, 128]],
                        compare_op=ALU.is_equal, fill=0.0, base=0,
                        channel_multiplier=1, tag="ident32")
        P.act(self.ident32, tmp32b, AF.Copy, tag="id32_c")
        P.afill(self.ones_row, 1.0, tag="ones")

    def c16(self, byte_off, width, p0=0, p1=128):
        assert byte_off % 2 == 0
        return Prog.carve(self.A, p0, p1, byte_off // 2, byte_off // 2 + width, BF16)

    def c32(self, byte_off, width, p0=0, p1=128):
        assert byte_off % 4 == 0
        return Prog.carve(self.A, p0, p1, byte_off // 4, byte_off // 4 + width, F32)

    def col(self, n=1):
        if self.col_n + n > 512:
            self.col_n = 0
        tv = Prog.carve(self.cols, 0, 128, self.col_n, self.col_n + n, F32)
        self.col_n += n
        return tv

    def ps16(self, ps, p0, p1, f0_bytes, width):
        return Prog.carve(ps, p0, p1, f0_bytes // 2, f0_bytes // 2 + width, BF16)



def _ln_into(P, cx, x32, out32, scratch, add_to=None, tag=""):
    ssum = cx.col()
    ssq = cx.col()
    P.act(scratch, x32, AF.Square, accum_out=ssq, tag=f"{tag}sq")
    P.reduce(ssum, x32, op=ALU.add, tag=f"{tag}sm")
    stats = cx.col(2)
    negmean = stats[:, 0:1]
    invstd = stats[:, 1:2]
    P.ts(negmean, ssum, -1.0 / D, None, op0=ALU.mult, tag=f"{tag}nm")
    var = cx.col()
    msq = cx.col()
    P.tt(msq, negmean, negmean, ALU.mult, tag=f"{tag}mq")
    P.stt(var, ssq, 1.0 / D, msq, ALU.mult, ALU.subtract, tag=f"{tag}vr")
    std = cx.col()
    P.act(std, var, AF.Sqrt, bias=EPS, tag=f"{tag}sd")
    P.recip(invstd, std, tag=f"{tag}ri")
    if add_to is None:
        P.ts(out32, x32, negmean, invstd, op0=ALU.add, op1=ALU.mult, tag=f"{tag}n")
    else:
        P.ts(scratch, x32, negmean, invstd, op0=ALU.add, op1=ALU.mult, tag=f"{tag}n")
        P.tt(out32, scratch, add_to, ALU.add, tag=f"{tag}a")
    return stats


def emit_boundary(P, cx, src, bo, r_out, xn_out, stats_out=None,
                  gamma_rep=None, beta_rep=None, prep_stats=None, tag="bnd"):
    for it in range(NT):
        g0 = it * 128
        row0 = g0 * D
        base = (it % 2) * 24576
        srctile = cx.c32(base + 0, D)
        botile = cx.c32(base + 4096, D)
        rnew = cx.c32(base + 8192, D)
        xn32 = cx.c32(base + 12288, D)
        scratch = cx.c32(base + 16384, D)
        xn16 = cx.c16(base + 20480, D)
        xnT = cx.c16(base + 22528, D)

        sv = src.flat(row0, row0 + 128 * D)
        P.dma("sp", srctile, (sv.ap2d(128, D), sv.region()), tag=f"{tag}.s{it}")
        if bo is not None:
            bv = bo.flat(row0, row0 + 128 * D)
            P.dma("act", botile, (bv.ap2d(128, D), bv.region()), tag=f"{tag}.o{it}")
            if prep_stats is not None:
                stt_tile = cx.col(2)
                so = prep_stats.flat(it * 256, it * 256 + 256)
                P.dma("act", stt_tile, (so.ap2d(128, 2), so.region()),
                      tag=f"{tag}.ps{it}")
                xpl = cx.c32(base + 16384, D)
                P.ts(xpl, srctile, stt_tile[:, 0:1], stt_tile[:, 1:2],
                     op0=ALU.add, op1=ALU.mult, tag=f"{tag}.px{it}")
                P.tt(botile, botile, xpl, ALU.add, tag=f"{tag}.pa{it}")
            _ln_into(P, cx, botile, rnew, scratch, add_to=srctile, tag=f"{tag}.u{it}")
        else:
            rnew = srctile
        if r_out is not None:
            rv = r_out.flat(row0, row0 + 128 * D)
            P.dma("sp", (rv.ap2d(128, D), rv.region()), rnew, tag=f"{tag}.r{it}")
        if xn_out is None:
            continue
        stats = _ln_into(P, cx, rnew, xn32, scratch, tag=f"{tag}.v{it}")
        if stats_out is not None:
            so = stats_out.flat(it * 256, it * 256 + 256)
            P.dma("act", (so.ap2d(128, 2), so.region()), stats, tag=f"{tag}.t{it}")
        if gamma_rep is not None:
            P.tt(xn32, xn32, gamma_rep, ALU.mult, tag=f"{tag}.g{it}")
            P.tt(xn32, xn32, beta_rep, ALU.add, tag=f"{tag}.e{it}")
        P.act(xn16, xn32, AF.Copy, tag=f"{tag}.c{it}")
        for k in range(8):
            pst = cx.ps16(cx.PSD, 0, 128, 0, 128)
            P.transpose(pst, xn16[:, k * 128:(k + 1) * 128], cx.ident,
                        tag=f"{tag}.T{it}_{k}")
            P.act(xnT[:, k * 128:(k + 1) * 128], pst, AF.Copy, tag=f"{tag}.E{it}_{k}")
        dst_ap = xn_out._flat_ap()[0:8 * 128 * 4096].rearrange(
            "(k p g) -> p k g", k=8, p=128)[:, :, g0:g0 + 128]
        P.dma("sp", (dst_ap, xn_out.region()), xnT, tag=f"{tag}.x{it}")


def make_rep_row(P, cx, src_dram, e0, n, name, tag):
    row = P.sbuf(f"row_{name}", 1, n, F32)
    out32 = P.sbuf(f"rep_{name}", 128, n, F32)
    sv = src_dram.flat(e0, e0 + n)
    P.dma("sp", row, (sv.ap2d(1, n), sv.region()), tag=f"{tag}.ld")
    for a, b in t_slices(n, 512):
        P.mm(Prog.carve(cx.PSC, 0, 128, a, b, F32), cx.ones_row[0:1, 0:128],
             row[0:1, a:b], tag=f"{tag}.mm{a}")
    P.copy(out32, Prog.carve(cx.PSC, 0, 128, 0, n, F32), tag=f"{tag}.cp")
    return out32


def cast_cache(P, w_tv, name):
    n = w_tv.p1
    wbf = P.dram(name, (n,), BF16)
    CH = 1 << 21
    for a, b in t_slices(n, CH):
        P.dma("pool", wbf.flat(a, b), (w_tv._flat_ap()[a:b], (w_tv.name, 0, 1, a, b)),
              tag=f"cc.{name}.{a}")
    return wbf


def load_wT(P, cx, wbf, rows, cols, r0, rN, c0, out_off, tag, queue="sp"):
    out = cx.c16(out_off, rN)
    src = wbf.flat(0, rows * cols).ap2d(rows, cols)[r0:r0 + rN, c0:c0 + 128]
    P.dma_transpose(queue, out, (src, (wbf.name, 0, 1, r0 * cols, (r0 + rN) * cols)),
                    tag=tag)
    return out


def emit_qscr(P, cx, xnscr_dram, qkvw_bf, qkv_b, qscr_d, tag="qs"):
    qbs = P.sbuf(f"qbs_{tag}", 128, 8, F32)
    for j in range(8):
        sv = qkv_b.flat(j * 128, (j + 1) * 128)
        P.dma("act", qbs[:, j:j + 1], (sv.ap2d(128, 1), sv.region()),
              tag=f"{tag}.b{j}")
    for b_ in range(NB):
        g0 = b_ * T
        xn_sb = []
        for k in range(8):
            tvx = cx.c16(L0A["xn"] + k * 4096, T)
            src = xnscr_dram._flat_ap()[0:8 * 128 * 4096].rearrange(
                "(k p g) -> k p g", k=8, p=128)[k, :, g0:g0 + T]
            P.dma("sp", tvx, (src, xnscr_dram.region()), tag=f"{tag}.x{b_}_{k}")
            xn_sb.append(tvx)
        wT = [load_wT(P, cx, qkvw_bf, 3072, 1024, 0, 1024, k * 128,
                      L0A["wT"] + k * 2048, tag=f"{tag}.w{b_}_{k}")
              for k in range(8)]
        for i in range(8):
            qtile = cx.c16(L0A["qT"] + i * 4096, T)
            for j in range(4):
                ps = Prog.carve(cx.PSA if (i * 4 + j) % 2 == 0 else cx.PSB,
                                0, 128, 0, 512, F32)
                for k in range(8):
                    P.mm(ps, wT[k][:, i * 128:(i + 1) * 128],
                         xn_sb[k][:, j * 512:(j + 1) * 512],
                         start=(k == 0), stop=(k == 7),
                         tag=f"{tag}.m{b_}_{i}_{j}_{k}")
                P.act(qtile[:, j * 512:(j + 1) * 512], ps, AF.Copy,
                      bias=qbs[:, i:i + 1], tag=f"{tag}.e{b_}_{i}_{j}")
            dst = qscr_d._flat_ap()[0:8 * 128 * 4096].rearrange(
                "(i p g) -> i p g", i=8, p=128)[i, :, g0:g0 + T]
            P.dma("sp", (dst, qscr_d.region()), qtile, tag=f"{tag}.o{b_}_{i}")



L0A = dict(xn=0, qT=32768, kT=65536, v=98304, otok=0, wT=131072, scr=147456)


def emit_l0_attn(P, cx, xn_dram, bo, qkvw_bf, projw_bf, rel_f32, qkv_b, proj_b,
                 f_dram, qscr_d, prefill_F=True, tag="l0a"):
    H = 16
    qscr_sb = P.sbuf(f"qscr_{tag}", 128, 2048, BF16)

    qb = P.sbuf(f"qb_{tag}", 128, 16, F32)
    for j in range(16):
        sv = qkv_b.flat(j * 128, (j + 1) * 128)
        P.dma("act", qb[:, j:j + 1], (sv.ap2d(128, 1), sv.region()), tag=f"{tag}.qb{j}")
    vb_row = P.sbuf(f"vb_{tag}", 1, 1024, F32)
    sv = qkv_b.flat(2048, 3072)
    P.dma("act", vb_row, (sv.ap2d(1, 1024), sv.region()), tag=f"{tag}.vb")
    pb_row = P.sbuf(f"pb_{tag}", 1, 1024, F32)
    sv = proj_b.flat(0, 1024)
    P.dma("act", pb_row, (sv.ap2d(1, 1024), sv.region()), tag=f"{tag}.pb")

    relT = [P.sbuf(f"relT{h}_{tag}", 128, 199, BF16) for h in range(H)]
    for h in range(H):
        r32a = cx.c32(L0A["scr"], 64, p0=0, p1=128)
        r32b = cx.c32(L0A["scr"] + 256, 64, p0=0, p1=71)
        sv = rel_f32.flat(h * 199 * 64, h * 199 * 64 + 128 * 64)
        P.dma("sp", r32a, (sv.ap2d(128, 64), sv.region()), tag=f"{tag}.r{h}a")
        sv = rel_f32.flat(h * 199 * 64 + 128 * 64, (h + 1) * 199 * 64)
        P.dma("sp", r32b, (sv.ap2d(71, 64), sv.region()), tag=f"{tag}.r{h}b")
        r16a = cx.c16(L0A["scr"] + 512, 64, p0=0, p1=128)
        r16b = cx.c16(L0A["scr"] + 640, 64, p0=0, p1=71)
        P.act(r16a, r32a, AF.Copy, tag=f"{tag}.rc{h}a")
        P.act(r16b, r32b, AF.Copy, tag=f"{tag}.rc{h}b")
        psa = cx.ps16(cx.PSE, 0, 64, 0, 128)
        psb = cx.ps16(cx.PSE, 0, 64, 256, 71)
        P.transpose(psa, r16a, cx.ident, tag=f"{tag}.rT{h}a")
        P.transpose(psb, r16b, cx.ident[0:71, 0:71], tag=f"{tag}.rT{h}b")
        P.act(relT[h][0:64, 0:128], psa, AF.Copy, tag=f"{tag}.rE{h}a")
        P.act(relT[h][0:64, 128:199], psb, AF.Copy, tag=f"{tag}.rE{h}b")
        P.act(relT[h][64:128, 0:128], psa, AF.Copy, tag=f"{tag}.rE{h}c")
        P.act(relT[h][64:128, 128:199], psb, AF.Copy, tag=f"{tag}.rE{h}d")

    if prefill_F:
        negt = cx.c32(L0A["scr"] + 1024, FPITCH)
        P.afill(negt, NEG)
        fsize = 2048 * FPITCH + 2048
        nfull = fsize // (128 * FPITCH)
        for i in range(nfull):
            a = i * 128 * FPITCH
            fv = f_dram.flat(a, a + 128 * FPITCH)
            P.dma("sp", (fv.ap2d(128, FPITCH), fv.region()), negt[:, :],
                  tag=f"{tag}.fp{i}")
        a = nfull * 128 * FPITCH
        rem = fsize - a
        rows = rem // FPITCH
        if rows:
            fv = f_dram.flat(a, a + rows * FPITCH)
            P.dma("sp", (fv.ap2d(rows, FPITCH), fv.region()), negt[0:rows, :],
                  tag=f"{tag}.fpl")
        a += rows * FPITCH
        if fsize - a:
            fv = f_dram.flat(a, fsize)
            P.dma("sp", (fv.ap2d(1, fsize - a), fv.region()), negt[0:1, 0:fsize - a],
                  tag=f"{tag}.fpr")

    for b_ in range(NB):
        _l0a_batch(P, cx, b_, xn_dram, bo, qkvw_bf, projw_bf, relT, qb, vb_row,
                   pb_row, f_dram, qscr_d, qscr_sb, tag=f"{tag}.b{b_}")


def _l0a_batch(P, cx, b_, xn_dram, bo, qkvw_bf, projw_bf, relT, qb, vb_row,
               pb_row, f_dram, qscr_d, qscr_sb, tag):
    H = 16
    g0 = b_ * T
    xn_sb = []
    for k in range(8):
        tvx = cx.c16(L0A["xn"] + k * 4096, T)
        src = xn_dram._flat_ap()[0:8 * 128 * 4096].rearrange(
            "(k p g) -> k p g", k=8, p=128)[k, :, g0:g0 + T]
        P.dma("sp", tvx, (src, xn_dram.region()), tag=f"{tag}.xl{k}")
        xn_sb.append(tvx)

    qT = [cx.c16(L0A["qT"] + k * 4096, T) for k in range(8)]
    kTt = [cx.c16(L0A["kT"] + k * 4096, T) for k in range(8)]
    v_sb = [cx.c16(L0A["v"] + i * 2048, 1024) for i in range(16)]

    for proj in range(3):
        wT = [load_wT(P, cx, qkvw_bf, 3072, 1024, proj * 1024, 1024, k * 128,
                      L0A["wT"] + k * 2048, tag=f"{tag}.w{proj}_{k}")
              for k in range(8)]
        if proj in (0, 1):
            dst = qT if proj == 0 else kTt
            for i in range(8):
                for j in range(4):
                    ps = Prog.carve(cx.PSA if (i * 4 + j) % 2 == 0 else cx.PSB,
                                    0, 128, 0, 512, F32)
                    for k in range(8):
                        P.mm(ps, wT[k][:, i * 128:(i + 1) * 128],
                             xn_sb[k][:, j * 512:(j + 1) * 512],
                             start=(k == 0), stop=(k == 7),
                             tag=f"{tag}.m{proj}_{i}_{j}_{k}")
                    P.act(dst[i][:, j * 512:(j + 1) * 512], ps, AF.Copy,
                          bias=qb[:, proj * 8 + i:proj * 8 + i + 1],
                          tag=f"{tag}.e{proj}_{i}_{j}")
        else:
            for t_ in range(16):
                for jh in range(2):
                    ps = Prog.carve(cx.PSA if (t_ * 2 + jh) % 2 == 0 else cx.PSB,
                                    0, 128, 0, 512, F32)
                    for k in range(8):
                        P.mm(ps, xn_sb[k][:, t_ * 128:(t_ + 1) * 128],
                             wT[k][:, jh * 512:(jh + 1) * 512],
                             start=(k == 0), stop=False,
                             tag=f"{tag}.vm{t_}_{jh}_{k}")
                    P.mm(ps, cx.ones_row[0:1, 0:128],
                         vb_row[0:1, jh * 512:(jh + 1) * 512],
                         start=False, stop=True, tag=f"{tag}.vb{t_}_{jh}")
                    P.act(v_sb[t_][:, jh * 512:(jh + 1) * 512], ps, AF.Copy,
                          tag=f"{tag}.ve{t_}_{jh}")

    otok = [cx.c16(L0A["otok"] + i * 2048, 1024) for i in range(16)]
    pbuf = cx.c32(L0A["scr"] + 4096, 256)
    for h in range(H):
        qtv = qT[h // 2]
        pbase = 64 * (h % 2)
        if h % 2 == 0:
            qsrc = qscr_d._flat_ap()[0:8 * 128 * 4096].rearrange(
                "(i p g) -> i p g", i=8, p=128)[h // 2, :, b_ * T:(b_ + 1) * T]
            P.dma("sp", qscr_sb, (qsrc, qscr_d.region()), tag=f"{tag}.qs{h}")
        for t_ in range(16):
            ps = Prog.carve(cx.PSD, 0, 128, 0, 199, F32)
            P.mm(ps, qscr_sb[pbase:pbase + 64, t_ * 128:(t_ + 1) * 128],
                 relT[h][pbase:pbase + 64, :], tag=f"{tag}.P{h}_{t_}")
            P.act(pbuf[:, 0:199], ps, AF.Copy, tag=f"{tag}.Q{h}_{t_}")
            fv = f_dram.flat(t_ * 128 * FPITCH, (t_ * 128 + 128) * FPITCH)
            P.dma("sp", (fv.ap2d(128, 199, row_pitch=FPITCH), fv.region()),
                  pbuf[:, 0:199], tag=f"{tag}.W{h}_{t_}")
        for t_ in range(16):
            qs = t_ * 128
            ks = max(0, qs - 128)
            ke = min(T, qs + 256)
            Kt = ke - ks
            nch = Kt // 128
            psS = Prog.carve(cx.PSD, 0, 128, 0, Kt, F32)
            P.mm(psS, qtv[pbase:pbase + 64, qs:qs + 128],
                 kTt[h // 2][pbase:pbase + 64, ks:ke], tag=f"{tag}.S{h}_{t_}")
            ttile = cx.c32(L0A["scr"] + 8192, Kt)
            fbase = qs * (FPITCH - 1) + ks + 99
            fv = f_dram.flat(fbase, fbase + 128 * (FPITCH - 1))
            P.dma("act", ttile, (fv.ap2d(128, Kt, row_pitch=FPITCH - 1), fv.region()),
                  tag=f"{tag}.L{h}_{t_}")
            s2 = cx.c32(L0A["scr"] + 10240, Kt)
            P.stt(s2, psS, 0.125, ttile, ALU.mult, ALU.add, tag=f"{tag}.t{h}_{t_}")
            nmax = cx.col()
            P.reduce(nmax, s2, op=ALU.max, negate=True, tag=f"{tag}.x{h}_{t_}")
            pr = cx.c32(L0A["scr"] + 12288, Kt)
            ssum = cx.col()
            P.act(pr, s2, AF.Exp, bias=nmax, accum_out=ssum, tag=f"{tag}.a{h}_{t_}")
            rinv = cx.col()
            P.recip(rinv, ssum, tag=f"{tag}.i{h}_{t_}")
            pr16 = cx.c16(L0A["scr"] + 14336, Kt)
            P.act(pr16, pr, AF.Copy, scale=rinv, tag=f"{tag}.n{h}_{t_}")
            psT = cx.ps16(cx.PSE, 0, 128, 0, 384)
            for c in range(nch):
                P.transpose(psT[:, c * 128:(c + 1) * 128],
                            pr16[:, c * 128:(c + 1) * 128], cx.ident,
                            tag=f"{tag}.c{h}_{t_}_{c}")
            pt16 = cx.c16(L0A["scr"] + 16384, 384)
            P.act(pt16[:, 0:nch * 128], psT[:, 0:nch * 128], AF.Copy, tag=f"{tag}.d{h}_{t_}")
            psO = Prog.carve(cx.PSA, 0, 128, 0, 64, F32)
            for c in range(nch):
                kt_idx = ks // 128 + c
                P.mm(psO, pt16[:, c * 128:(c + 1) * 128],
                     v_sb[kt_idx][:, 64 * h:64 * h + 64],
                     start=(c == 0), stop=(c == nch - 1), tag=f"{tag}.v{h}_{t_}_{c}")
            P.act(otok[t_][:, 64 * h:64 * h + 64], psO, AF.Copy, tag=f"{tag}.o{h}_{t_}")

    pwT = [load_wT(P, cx, projw_bf, 1024, 1024, 0, 1024, k * 128,
                   L0A["wT"] + k * 2048, tag=f"{tag}.pw{k}") for k in range(8)]
    obuf = cx.c32(L0A["scr"] + 18432, D)
    oTt = cx.c16(L0A["scr"] + 8192, 128)
    for t_ in range(16):
        psP = Prog.carve(cx.PSB, 0, 128, 0, 1024, F32)
        first = True
        for k in range(8):
            pst = cx.ps16(cx.PSE, 0, 128, 0, 128)
            P.transpose(pst, otok[t_][:, k * 128:(k + 1) * 128], cx.ident,
                        tag=f"{tag}.f{t_}_{k}")
            P.act(oTt, pst, AF.Copy, tag=f"{tag}.g{t_}_{k}")
            for jh in range(2):
                P.mm(Prog.carve(cx.PSB, 0, 128, jh * 512, (jh + 1) * 512, F32),
                     oTt, pwT[k][:, jh * 512:(jh + 1) * 512],
                     start=(k == 0), stop=False, tag=f"{tag}.h{t_}_{k}_{jh}")
        for jh in range(2):
            P.mm(Prog.carve(cx.PSB, 0, 128, jh * 512, (jh + 1) * 512, F32),
                 cx.ones_row[0:1, 0:128], pb_row[0:1, jh * 512:(jh + 1) * 512],
                 start=False, stop=True, tag=f"{tag}.j{t_}_{jh}")
        P.act(obuf, psP, AF.Copy, tag=f"{tag}.k{t_}")
        bv = bo.flat((g0 + t_ * 128) * D, (g0 + t_ * 128 + 128) * D)
        P.dma("sp", (bv.ap2d(128, D), bv.region()), obuf, tag=f"{tag}.w{t_}")


L0F = dict(xn=0, bd1=8192, bd2=16384, yT=49152, hT=73728, cwT=75264, out=81920,
           stg=86016)


def emit_l0f_bdc_image(P, cx, rm_d, rup_d, tag="bdc"):
    bdc = P.dram("BDCi", (32 * 128 * 512,), BF16)
    z16 = cx.c16(L0F["stg"], 512)
    P.amemzero(z16)
    for hg in range(32):
        fv = bdc.flat(hg * 65536, (hg + 1) * 65536)
        P.dma("sp", (fv.ap2d(128, 512), fv.region()), z16[0:128, :], tag=f"{tag}.z{hg}")
    for h in range(4):
        for g in range(8):
            hg = h * 8 + g
            for a in range(8):
                r = 8 * g + a
                par = (a % 2) * 4096
                rm_s = cx.c32(L0F["stg"] + par + 1024, 16, p0=0, p1=16)
                rup_s = cx.c32(L0F["stg"] + par + 1088, 16, p0=0, p1=64)
                sv = rm_d.flat((h * 64 + r) * 256, (h * 64 + r) * 256 + 256)
                P.dma("pool", rm_s, (sv.ap2d(16, 16), sv.region()), tag=f"{tag}.m{hg}{a}")
                sv = rup_d.flat((h * 64 + r) * 1024, (h * 64 + r) * 1024 + 1024)
                P.dma("pool", rup_s, (sv.ap2d(64, 16), sv.region()), tag=f"{tag}.u{hg}{a}")
                ps1 = Prog.carve(cx.PSE, 0, 16, 0, 16, F32)
                P.transpose(ps1, rm_s, cx.ident32[0:16, 0:16], tag=f"{tag}.t1{hg}{a}")
                rmT = cx.c32(L0F["stg"] + par + 1536, 16, p0=0, p1=16)
                P.act(rmT, ps1, AF.Copy, tag=f"{tag}.e1{hg}{a}")
                ps2 = Prog.carve(cx.PSE, 0, 16, 64, 128, F32)
                P.transpose(ps2, rup_s, cx.ident32[0:64, 0:64], tag=f"{tag}.t2{hg}{a}")
                rupT = cx.c32(L0F["stg"] + par + 1600, 64, p0=0, p1=16)
                P.act(rupT, ps2, AF.Copy, tag=f"{tag}.e2{hg}{a}")
                psC = Prog.carve(cx.PSD, 0, 16, 0, 64, F32)
                P.mm(psC, rmT, rupT, tag=f"{tag}.c{hg}{a}")
                c16 = cx.c16(L0F["stg"] + par + 2048, 64, p0=0, p1=16)
                P.act(c16, psC, AF.Copy, tag=f"{tag}.ec{hg}{a}")
                base = hg * 65536
                dsta = bdc.flat(base + 16 * a * 512 + 64 * a,
                                base + (16 * a + 15) * 512 + 64 * a + 64)
                P.dma("sp", (dsta.ap2d(16, 64, row_pitch=512), dsta.region()),
                      c16, tag=f"{tag}.w{hg}{a}")
    return bdc


def emit_l0_ffn(P, cx, xn_dram, bo, bdc, cw_bf, tag="l0f"):
    bdc_sb = [cx.c16(L0F["bd2"] + hg * 1024, 512) for hg in range(32)]
    for hg in range(32):
        fv = bdc.flat(hg * 65536, (hg + 1) * 65536)
        P.dma("sp", bdc_sb[hg], (fv.ap2d(128, 512), fv.region()), tag=f"{tag}.lb{hg}")

    tgs = t_slices(G, 384)
    for tgi, (ga, gb) in enumerate(tgs):
        tw = gb - ga
        nsub = (tw + 127) // 128
        src = xn_dram._flat_ap()[0:8 * 128 * 4096].rearrange(
            "(k p g) -> p k g", k=8, p=128)[:, :, ga:gb]
        P.dma("sp", Prog.carve(cx.A, 0, 128, 0, 8 * tw, BF16),
              (src, xn_dram.region()), tag=f"{tag}.x{tgi}")
        xk = [Prog.carve(cx.A, 0, 128, k * tw, (k + 1) * tw, BF16) for k in range(8)]
        pc = [cx.PSA, cx.PSB, cx.PSC]
        for ft in range(128):
            hg, m = ft // 4, ft % 4
            cwT = cx.c16(L0F["cwT"] + (ft % 2) * 2048, 1024)
            src = cw_bf.flat(0, 1024 * 16384).ap2d(1024, 16384)[:, ft * 128:(ft + 1) * 128]
            P.dma_transpose("sp", cwT, (src, (cw_bf.name, 0, 1, 0, 1024 * 16384)),
                            tag=f"{tag}.cw_{tgi}_{ft}")
            psZ = Prog.carve(cx.PSE if ft % 2 == 0 else cx.PSD, 0, 128, 0, tw, F32)
            P.mm(psZ, bdc_sb[hg][:, m * 128:(m + 1) * 128], xk[hg % 8],
                 tag=f"{tag}.s2_{tgi}_{ft}")
            hT = cx.c16(L0F["hT"] + (ft % 2) * 768, tw)
            P.act(hT, psZ, AF.Gelu, tag=f"{tag}.g_{tgi}_{ft}")
            for sub in range(nsub):
                s0 = sub * 128
                s1 = min(tw, s0 + 128)
                for jh in range(2):
                    P.mm(Prog.carve(pc[sub], 0, 128, jh * 512, (jh + 1) * 512, F32),
                         hT[:, s0:s1], cwT[:, jh * 512:(jh + 1) * 512],
                         start=(ft == 0), stop=(ft == 127),
                         tag=f"{tag}.c_{tgi}_{ft}_{sub}_{jh}")
        for sub in range(nsub):
            s0 = sub * 128
            s1 = min(tw, s0 + 128)
            P.act(Prog.carve(cx.A, 0, s1 - s0, L0F["out"] // 4, L0F["out"] // 4 + D, F32),
                  Prog.carve(pc[sub], 0, s1 - s0, 0, D, F32), AF.Copy,
                  tag=f"{tag}.oe{tgi}_{sub}")
            bv = bo.flat((ga + s0) * D, (ga + s1) * D)
            P.dma("sp", (bv.ap2d(s1 - s0, D), bv.region()),
                  Prog.carve(cx.A, 0, s1 - s0, L0F["out"] // 4, L0F["out"] // 4 + D, F32),
                  tag=f"{tag}.ow{tgi}_{sub}")


L1A = dict(xn=0, wT=32768, e=49152, c=81920, tA=114688, tB=122880,
           q2=131072, g1=163840, scr=4096)


DBG = {}


def emit_l1_attn(P, cx, xn_dram, bo, wc, gnw_d, tag="l1a"):
    gn_rep = make_rep_row(P, cx, gnw_d, 0, 1024, f"gn_{tag}", f"{tag}.gn")
    ut_pre = cx.c16(L1A["scr"] + 16384, 64, p0=0, p1=64)
    ut_raw = cx.c16(L1A["scr"] + 16640, 64, p0=0, p1=64)
    ut = P.sbuf(f"ut_{tag}", 64, 64, F32)
    P.afill(ut_pre, 1.0, tag=f"{tag}.utp")
    P.affine_select(ut_raw, ut_pre, pattern=[[1, 64]], compare_op=ALU.is_ge,
                    fill=0.0, base=0, channel_multiplier=-1, tag=f"{tag}.uts")
    P.act(ut, ut_raw, AF.Copy, tag=f"{tag}.utc")
    shared = dict(
        cbs=P.sbuf(f"cb_{tag}", 128, 8 * 33, F32),
        rcbs=P.sbuf(f"rcb_{tag}", 128, 8 * 33, F32),
        S32=P.sbuf(f"S32_{tag}", 128, 8 * 128, F32),
        S16=P.sbuf(f"S16_{tag}", 128, 8 * 128, BF16),
    )
    for b_ in range(NB):
        _l1a_batch(P, cx, b_, xn_dram, bo, wc, gn_rep, ut, shared, tag=f"{tag}.b{b_}")


def _l1a_batch(P, cx, b_, xn_dram, bo, wc, gn_rep, ut, shared, tag):
    H = 8
    g0 = b_ * T
    v_dram = P.dram(f"vtmp_{tag}", (T * 1024,), BF16)
    xn_sb = []
    for k in range(8):
        tvx = cx.c16(L1A["xn"] + k * 4096, T)
        src = xn_dram._flat_ap()[0:8 * 128 * 4096].rearrange(
            "(k p g) -> k p g", k=8, p=128)[k, :, g0:g0 + T]
        P.dma("sp", tvx, (src, xn_dram.region()), tag=f"{tag}.xl{k}")
        xn_sb.append(tvx)

    e16 = [cx.c16(L1A["e"] + i * 4096, T) for i in range(8)]
    c16 = [cx.c16(L1A["c"] + i * 4096, T) for i in range(8)]
    q2 = [cx.c16(L1A["q2"] + i * 4096, T) for i in range(8)]
    g1 = cx.c16(L1A["g1"], T)
    cbs = shared["cbs"]
    rcbs = shared["rcbs"]

    sc1 = cx.c32(L1A["tB"], T)
    P.memset(sc1, 1.0, tag=f"{tag}.s1")

    wT = [load_wT(P, cx, wc["Wk"], 1024, 1024, 0, 1024, k * 128,
                  L1A["wT"] + k * 2048, tag=f"{tag}.wk{k}") for k in range(8)]
    for i in range(8):
        kr = cx.c32(L1A["tA"], T)
        for j in range(4):
            ps = Prog.carve(cx.PSA if j % 2 == 0 else cx.PSB, 0, 128, 0, 512, F32)
            for k in range(8):
                P.mm(ps, wT[k][:, i * 128:(i + 1) * 128],
                     xn_sb[k][:, j * 512:(j + 1) * 512],
                     start=(k == 0), stop=(k == 7), tag=f"{tag}.km{i}_{j}_{k}")
            P.copy(kr[:, j * 512:(j + 1) * 512], ps, tag=f"{tag}.ke{i}_{j}")
        nmax = cx.col()
        P.reduce(nmax, kr, op=ALU.max, negate=True, tag=f"{tag}.kx{i}")
        P.act(e16[i], kr, AF.Exp, bias=nmax, tag=f"{tag}.kp{i}")
        P.scan(c16[i], e16[i], sc1, 0.0, ALU.add, ALU.mult, tag=f"{tag}.kc{i}")
        P.copy(cbs[:, i * 33:i * 33 + 1], c16[i][:, 0:1], tag=f"{tag}.kb{i}")
        strided = cx_strided(c16[i], 63, 64, 32)
        P.raw("dve", lambda e, o=cbs[:, i * 33 + 1:i * 33 + 33].ap(), s=strided:
              e.tensor_copy(o, s), reads=[c16[i]],
              writes=[cbs[:, i * 33 + 1:i * 33 + 33]], tag=f"{tag}.kb2{i}")
        P.recip(rcbs[:, i * 33:(i + 1) * 33], cbs[:, i * 33:(i + 1) * 33],
                tag=f"{tag}.kr{i}")

    wT = [load_wT(P, cx, wc["Wq"], 1024, 1024, 0, 1024, k * 128,
                  L1A["wT"] + k * 2048, tag=f"{tag}.wq{k}") for k in range(8)]
    for i in range(8):
        qf = cx.c32(L1A["tA"] if i % 2 == 0 else L1A["tB"], T)
        for j in range(4):
            ps = Prog.carve(cx.PSA if j % 2 == 0 else cx.PSB, 0, 128, 0, 512, F32)
            for k in range(8):
                P.mm(ps, wT[k][:, i * 128:(i + 1) * 128],
                     xn_sb[k][:, j * 512:(j + 1) * 512],
                     start=(k == 0), stop=(k == 7), tag=f"{tag}.qm{i}_{j}_{k}")
            P.act(qf[:, j * 512:(j + 1) * 512], ps, AF.Silu, tag=f"{tag}.qs{i}_{j}")
        q2f = cx.c32(L1A["tB"] if i % 2 == 0 else L1A["tA"], T)
        for n in range(32):
            rct = cx.c32(L1A["scr"] + 43008, 64, p0=0, p1=128)
            P.recip(rct, c16[i][:, n * 64:(n + 1) * 64], tag=f"{tag}.qr{i}_{n}")
            cpcol = cbs[:, i * 33 + (0 if n == 0 else n):i * 33 + (0 if n == 0 else n) + 1]
            P.stt(q2f[:, n * 64:(n + 1) * 64], rct, cpcol,
                  qf[:, n * 64:(n + 1) * 64], ALU.mult, ALU.mult,
                  tag=f"{tag}.q2_{i}_{n}")
        P.act(q2[i], q2f, AF.Copy, tag=f"{tag}.qc{i}")

    wT = [load_wT(P, cx, wc["Wv"], 1024, 1024, 0, 1024, k * 128,
                  L1A["wT"] + k * 2048, tag=f"{tag}.wv{k}") for k in range(8)]
    for t_ in range(16):
        vtile = cx.c16(L1A["tA"] + (t_ % 2) * 8192, 1024)
        for jh in range(2):
            ps = Prog.carve(cx.PSA if (t_ + jh) % 2 == 0 else cx.PSB, 0, 128,
                            0, 512, F32)
            for k in range(8):
                P.mm(ps, xn_sb[k][:, t_ * 128:(t_ + 1) * 128],
                     wT[k][:, jh * 512:(jh + 1) * 512],
                     start=(k == 0), stop=(k == 7), tag=f"{tag}.vm{t_}_{jh}_{k}")
            P.act(vtile[:, jh * 512:(jh + 1) * 512], ps, AF.Copy,
                  tag=f"{tag}.vee{t_}_{jh}")
        dv = v_dram.flat(t_ * 128 * 1024, (t_ + 1) * 128 * 1024)
        P.dma("sp", (dv.ap2d(128, 1024), dv.region()), vtile, tag=f"{tag}.vw{t_}")

    wT1 = [load_wT(P, cx, wc["Wg1"], 128, 1024, 0, 128, k * 128,
                   L1A["wT"] + k * 2048, tag=f"{tag}.wg1{k}") for k in range(8)]
    for j in range(4):
        ps = Prog.carve(cx.PSA if j % 2 == 0 else cx.PSB, 0, 128, 0, 512, F32)
        for k in range(8):
            P.mm(ps, wT1[k], xn_sb[k][:, j * 512:(j + 1) * 512],
                 start=(k == 0), stop=(k == 7), tag=f"{tag}.gm{j}_{k}")
        P.act(g1[:, j * 512:(j + 1) * 512], ps, AF.Copy, tag=f"{tag}.ge{j}")

    woT = [load_wT(P, cx, wc["Wo"], 1024, 1024, 0, 1024, k * 128,
                   L1A["wT"] + k * 2048, tag=f"{tag}.wo{k}") for k in range(8)]
    wg2T = cx.c16(167936, 1024)
    src = wc["Wg2"].flat(0, 1024 * 128).ap2d(1024, 128)[:, 0:128]
    P.dma_transpose("sp", wg2T, (src, (wc["Wg2"].name, 0, 1, 0, 1024 * 128)),
                    tag=f"{tag}.wg2")

    S32 = shared["S32"]
    S16 = shared["S16"]
    P.amemzero(S32, tag=f"{tag}.sz")
    P.amemzero(S16, tag=f"{tag}.sz2")

    for n in range(32):
        par = n % 2
        vtok = Prog.carve(cx.A, 0, 64, (L1A["xn"] + par * 2048) // 2,
                          (L1A["xn"] + par * 2048) // 2 + 1024, BF16)
        dv = v_dram.flat(n * 64 * 1024, (n + 1) * 64 * 1024)
        P.dma("act", vtok, (dv.ap2d(64, 1024), dv.region()), tag=f"{tag}.lv{n}")
        psO = Prog.carve(cx.PSA, 0, 64, 0, 1024, F32)
        for h in range(8):
            cp = cbs[:, h * 33 + (0 if n == 0 else n):h * 33 + (0 if n == 0 else n) + 1]
            rcp = rcbs[:, h * 33 + (0 if n == 0 else n):h * 33 + (0 if n == 0 else n) + 1]
            rce = rcbs[:, h * 33 + n + 1:h * 33 + n + 2]
            ech = e16[h][:, n * 64:(n + 1) * 64]
            k2t = cx.c16(L1A["scr"] + par * 16384, 64)
            P.act(k2t, ech, AF.Copy, scale=rcp, tag=f"{tag}.k2_{n}_{h}")
            psA = Prog.carve(cx.PSD, 0, 64, 0, 64, F32)
            P.mm(psA, k2t, q2[h][:, n * 64:(n + 1) * 64], tag=f"{tag}.am_{n}_{h}")
            atf = cx.c32(L1A["scr"] + par * 16384 + 256, 64, p0=0, p1=64)
            P.tt(atf, psA, ut, ALU.mult, tag=f"{tag}.ms_{n}_{h}")
            at16 = cx.c16(L1A["scr"] + par * 16384 + 1024, 64, p0=0, p1=64)
            P.act(at16, atf, AF.Copy, tag=f"{tag}.ac_{n}_{h}")
            ktt = cx.c16(L1A["scr"] + par * 16384 + 1280, 64)
            P.act(ktt, ech, AF.Copy, scale=rce, tag=f"{tag}.kt_{n}_{h}")
            psKT = cx.ps16(cx.PSD, 0, 64, 512, 128)
            P.transpose(psKT, ktt, cx.ident[0:128, 0:128], tag=f"{tag}.ktT_{n}_{h}")
            kttok = cx.c16(L1A["scr"] + par * 16384 + 1536, 128, p0=0, p1=64)
            P.act(kttok, psKT, AF.Copy, tag=f"{tag}.kte_{n}_{h}")
            if n == 0 and h == 0:
                DBG["at16"] = at16
                DBG["kttok"] = kttok
            P.mm(psO[:, h * 128:(h + 1) * 128], q2[h][:, n * 64:(n + 1) * 64],
                 S16[:, h * 128:(h + 1) * 128], start=True, stop=False,
                 tag=f"{tag}.oi_{n}_{h}")
            P.mm(psO[:, h * 128:(h + 1) * 128], at16,
                 vtok[:, h * 128:(h + 1) * 128], start=False, stop=True,
                 tag=f"{tag}.oo_{n}_{h}")
            psU = Prog.carve(cx.PSB, 0, 128, h * 128, (h + 1) * 128, F32)
            P.mm(psU, kttok, vtok[:, h * 128:(h + 1) * 128], tag=f"{tag}.su_{n}_{h}")
            decay = cx.col()
            P.tt(decay, cp, rce, ALU.mult, tag=f"{tag}.dc_{n}_{h}")
            P.stt(S32[:, h * 128:(h + 1) * 128], S32[:, h * 128:(h + 1) * 128],
                  decay, psU, ALU.mult, ALU.add, tag=f"{tag}.sn_{n}_{h}")
            P.act(S16[:, h * 128:(h + 1) * 128], S32[:, h * 128:(h + 1) * 128],
                  AF.Copy, tag=f"{tag}.sc_{n}_{h}")
        if DBG.get("d") is not None and n == 0 and "b1" in tag:
            dbg = DBG["d"]
            stg2 = cx.c32(L1A["tB"] + 4096, 1024, p0=0, p1=64)
            P.copy(stg2, psO, tag="dbg_psO_cp")
            dv = dbg.flat(0, 64 * 1024)
            P.dma("sp", (dv.ap2d(64, 1024), dv.region()), stg2, tag="dbg_psO")
            P.act(stg2[0:64, 0:128], S16[0:64, 0:128], AF.Copy, tag="dbg_s16c")
            dv = dbg.flat(64 * 1024, 64 * 1024 + 64 * 128)
            P.dma("sp", (dv.ap2d(64, 128), dv.region()), stg2[0:64, 0:128], tag="dbg_s16")
            P.act(stg2[0:64, 0:64], DBG["at16"], AF.Copy, tag="dbg_atc")
            dv = dbg.flat(64 * 1024 + 64 * 128, 64 * 1024 + 64 * 128 + 64 * 64)
            P.dma("sp", (dv.ap2d(64, 64), dv.region()), stg2[0:64, 0:64], tag="dbg_at")
            P.act(stg2[0:64, 0:128], DBG["kttok"], AF.Copy, tag="dbg_ktc")
            dv = dbg.flat(64 * 1024 + 64 * 128 + 64 * 64,
                          64 * 1024 + 64 * 128 + 64 * 64 + 64 * 128)
            P.dma("sp", (dv.ap2d(64, 128), dv.region()), stg2[0:64, 0:128], tag="dbg_kt")
        osq = cx.c32(L1A["tA"], 1024, p0=0, p1=64)
        ssq = cx.col()[0:64, :]
        P.act(osq, psO, AF.Square, accum_out=ssq, tag=f"{tag}.rq{n}")
        rstd = cx.col()[0:64, :]
        sq = cx.col()[0:64, :]
        P.ts(sq, ssq, 1.0 / 1024.0, None, op0=ALU.mult, tag=f"{tag}.rm{n}")
        sqs = cx.col()[0:64, :]
        P.act(sqs, sq, AF.Sqrt, bias=EPS, tag=f"{tag}.rs{n}")
        P.recip(rstd, sqs, tag=f"{tag}.rr{n}")
        o_n = cx.c32(L1A["tA"] + 4096, 1024, p0=0, p1=64)
        P.stt(o_n, psO, rstd, gn_rep[0:64, :], ALU.mult, ALU.mult,
              tag=f"{tag}.rn{n}")
        psG = Prog.carve(cx.PSC, 0, 64, 0, 1024, F32)
        for jh in range(2):
            P.mm(Prog.carve(cx.PSC, 0, 64, jh * 512, (jh + 1) * 512, F32),
                 g1[:, n * 64:(n + 1) * 64], wg2T[:, jh * 512:(jh + 1) * 512],
                 tag=f"{tag}.g2_{n}_{jh}")
        g2s = cx.c32(L1A["tB"], 1024, p0=0, p1=64)
        P.act(g2s, psG, AF.Silu, tag=f"{tag}.gs{n}")
        rs32 = cx.c32(L1A["tB"] + 4096, 1024, p0=0, p1=64)
        P.tt(rs32, o_n, g2s, ALU.mult, tag=f"{tag}.rx{n}")
        rs16 = cx.c16(L1A["scr"] + 20480, 1024, p0=0, p1=64)
        P.act(rs16, rs32, AF.Copy, tag=f"{tag}.rc{n}")
        psW = Prog.carve(cx.PSC, 0, 64, 0, 1024, F32)
        for k in range(8):
            psT = cx.ps16(cx.PSD, 0, 128, 1024, 64)
            P.transpose(psT, rs16[:, k * 128:(k + 1) * 128], cx.ident[0:64, 0:64],
                        tag=f"{tag}.wt{n}_{k}")
            rsT = cx.c16(L1A["scr"] + 22528, 64)
            P.act(rsT, psT, AF.Copy, tag=f"{tag}.we{n}_{k}")
            for jh in range(2):
                P.mm(Prog.carve(cx.PSC, 0, 64, jh * 512, (jh + 1) * 512, F32),
                     rsT, woT[k][:, jh * 512:(jh + 1) * 512],
                     start=(k == 0), stop=(k == 7), tag=f"{tag}.wm{n}_{k}_{jh}")
        ob = cx.c32(L1A["tA"], 1024, p0=0, p1=64)
        P.act(ob, psW, AF.Copy, tag=f"{tag}.ob{n}")
        if DBG.get("d2") is not None and n == 0 and "b1" in tag:
            d2 = DBG["d2"]
            wdump = cx.c32(L1A["tB"], 1024, p0=0, p1=128)
            P.copy(wdump, woT[0], tag="dbgw_cp")
            dv = DBG["d3"].flat(0, 128 * 1024)
            P.dma("sp", (dv.ap2d(128, 1024), dv.region()), wdump, tag="dbgw")
            P.copy(wdump[:, 0:64], rsT, tag="dbgr_cp")
            dv = DBG["d3"].flat(128 * 1024, 128 * 1024 + 128 * 64)
            P.dma("sp", (dv.ap2d(128, 64), dv.region()), wdump[:, 0:64], tag="dbgr")
            for si, tvv in enumerate([o_n, g2s, rs32, ob]):
                dv = d2.flat(si * 65536, (si + 1) * 65536)
                P.dma("sp", (dv.ap2d(64, 1024), dv.region()), tvv, tag=f"dbg2_{si}")
        bv = bo.flat((g0 + n * 64) * D, (g0 + (n + 1) * 64) * D)
        P.dma("sp", (bv.ap2d(64, D), bv.region()), ob, tag=f"{tag}.bw{n}")


def cx_strided(tv, start, step, count):
    import dataclasses
    a = tv.ap()
    return dataclasses.replace(a, offset=a.offset + start,
                               ap=[[a.ap[0][0], a.ap[0][1]], [step, count]])


L1F = dict(w1=0, w2=65536, xn=131072, hT=143360, ytmp=145408, scr=153600)


def emit_l1_ffn(P, cx, xn_dram, moe_d, wg_d, w1_d, b1_d, w2_d, b2_d, tag="l1f"):
    wg_sb = cx.c32(L1F["scr"], 1024, p0=0, p1=8)
    sv = wg_d.flat(0, 8 * 1024)
    P.dma("sp", wg_sb, (sv.ap2d(8, 1024), sv.region()), tag=f"{tag}.wgl")
    wg16 = cx.c16(L1F["scr"] + 4096, 1024, p0=0, p1=8)
    P.act(wg16, wg_sb, AF.Copy, tag=f"{tag}.wgc")
    wgT = [cx.c16(L1F["scr"] + 6144 + k * 16, 8) for k in range(8)]
    for k in range(8):
        pst = cx.ps16(cx.PSD, 0, 128, 0, 8)
        P.transpose(pst, wg16[0:8, k * 128:(k + 1) * 128], cx.ident[0:8, 0:8],
                    tag=f"{tag}.wgT{k}")
        P.act(wgT[k], pst, AF.Copy, tag=f"{tag}.wgE{k}")
    G = P.sbuf(f"G_{tag}", 128, NT * 8, F32)
    for it in range(NT):
        g0 = it * 128
        xsl = cx.c16(L1F["scr"] + 8192 + (it % 2) * 2048, 8 * 128)
        src = xn_dram._flat_ap()[0:8 * 128 * 4096].rearrange(
            "(k p g) -> p k g", k=8, p=128)[:, :, g0:g0 + 128]
        P.dma("sp", xsl, (src, xn_dram.region()), tag=f"{tag}.gx{it}")
        psg = Prog.carve(cx.PSE, 0, 128, 0, 8, F32)
        for k in range(8):
            P.mm(psg, Prog.carve(cx.A, 0, 128,
                                 (L1F["scr"] + 8192 + (it % 2) * 2048) // 2 + k * 128,
                                 (L1F["scr"] + 8192 + (it % 2) * 2048) // 2 + (k + 1) * 128,
                                 BF16),
                 wgT[k], start=(k == 0), stop=(k == 7), tag=f"{tag}.gm{it}_{k}")
        lg = cx.c32(L1F["scr"] + 12288, 8)
        P.copy(lg, psg, tag=f"{tag}.ge{it}")
        nmx = cx.col()
        P.reduce(nmx, lg, op=ALU.max, negate=True, tag=f"{tag}.gq{it}")
        pr = cx.c32(L1F["scr"] + 12320, 8)
        ssum = cx.col()
        P.act(pr, lg, AF.Exp, bias=nmx, accum_out=ssum, tag=f"{tag}.gp{it}")
        rs = cx.col()
        P.recip(rs, ssum, tag=f"{tag}.gr{it}")
        probs = cx.c32(L1F["scr"] + 12352, 8)
        P.ts(probs, pr, rs, None, op0=ALU.mult, tag=f"{tag}.gn{it}")
        m1 = cx.col()
        P.reduce(m1, probs, op=ALU.max, tag=f"{tag}.g1{it}")
        is1 = cx.c32(L1F["scr"] + 12384, 8)
        P.ts(is1, probs, m1, None, op0=ALU.is_ge, tag=f"{tag}.gi{it}")
        p2 = cx.c32(L1F["scr"] + 12416, 8)
        P.stt(p2, is1, -1e30, probs, ALU.mult, ALU.add, tag=f"{tag}.g2{it}")
        m2 = cx.col()
        P.reduce(m2, p2, op=ALU.max, tag=f"{tag}.g3{it}")
        den = cx.col()
        P.tt(den, m1, m2, ALU.add, tag=f"{tag}.g4{it}")
        rden = cx.col()
        dn2 = cx.col()
        P.ts(dn2, den, 1e-9, None, op0=ALU.add, tag=f"{tag}.g5{it}")
        P.recip(rden, dn2, tag=f"{tag}.g6{it}")
        mask = cx.c32(L1F["scr"] + 12448, 8)
        P.ts(mask, probs, m2, None, op0=ALU.is_ge, tag=f"{tag}.g7{it}")
        gm = cx.c32(L1F["scr"] + 12480, 8)
        P.tt(gm, probs, mask, ALU.mult, tag=f"{tag}.g8{it}")
        P.ts(G[:, it * 8:(it + 1) * 8], gm, rden, None, op0=ALU.mult,
             tag=f"{tag}.g9{it}")

    b1sb = P.sbuf(f"b1_{tag}", 128, 32, F32)
    b2row = P.sbuf(f"b2_{tag}", 1, 1024, F32)
    w1_sb = [cx.c16(L1F["w1"] + k * 8192, 4096) for k in range(8)]
    w2_sb = [cx.c16(L1F["w2"] + k * 2048, 1024) for k in range(32)]
    tgs = t_slices(G_TOK, 384)
    pc = [cx.PSA, cx.PSB, cx.PSC]
    for e in range(8):
        for k in range(8):
            ap, reg = w1_d.nd(e, slice(k * 128, (k + 1) * 128), slice(None))
            P.dma("pool", w1_sb[k], (ap, reg), tag=f"{tag}.w1_{e}_{k}")
        for k in range(32):
            ap, reg = w2_d.nd(e, slice(k * 128, (k + 1) * 128), slice(None))
            P.dma("pool", w2_sb[k], (ap, reg), tag=f"{tag}.w2_{e}_{k}")
        for j in range(32):
            sv = b1_d.flat(e * 4096 + j * 128, e * 4096 + (j + 1) * 128)
            P.dma("act", b1sb[:, j:j + 1], (sv.ap2d(128, 1), sv.region()),
                  tag=f"{tag}.b1_{e}_{j}")
        sv = b2_d.flat(e * 1024, (e + 1) * 1024)
        P.dma("act", b2row, (sv.ap2d(1, 1024), sv.region()), tag=f"{tag}.b2_{e}")
        for tgi, (ga, gb) in enumerate(tgs):
            tw = gb - ga
            nsub = (tw + 127) // 128
            xbase = L1F["xn"] + (tgi % 2) * 6144
            src = xn_dram._flat_ap()[0:8 * 128 * 4096].rearrange(
                "(k p g) -> p k g", k=8, p=128)[:, :, ga:gb]
            P.dma("sp", Prog.carve(cx.A, 0, 128, xbase // 2, xbase // 2 + 8 * tw, BF16),
                  (src, xn_dram.region()), tag=f"{tag}.x{e}_{tgi}")
            xk = [Prog.carve(cx.A, 0, 128, xbase // 2 + k * tw,
                             xbase // 2 + (k + 1) * tw, BF16) for k in range(8)]
            for ft in range(32):
                psH = Prog.carve(cx.PSD if ft % 2 == 0 else cx.PSE, 0, 128, 0, tw, F32)
                for k in range(8):
                    P.mm(psH, w1_sb[k][:, ft * 128:(ft + 1) * 128], xk[k],
                         start=(k == 0), stop=(k == 7), tag=f"{tag}.h{e}_{tgi}_{ft}_{k}")
                hT = cx.c16(L1F["hT"] + (ft % 2) * 768, tw)
                P.act(hT, psH, AF.Gelu_apprx_tanh, bias=b1sb[:, ft:ft + 1],
                      tag=f"{tag}.g{e}_{tgi}_{ft}")
                for sub in range(nsub):
                    s0 = sub * 128
                    s1 = min(tw, s0 + 128)
                    for jh in range(2):
                        P.mm(Prog.carve(pc[sub], 0, s1 - s0, jh * 512, (jh + 1) * 512, F32),
                             hT[:, s0:s1], w2_sb[ft][:, jh * 512:(jh + 1) * 512],
                             start=(ft == 0), stop=False,
                             tag=f"{tag}.y{e}_{tgi}_{ft}_{sub}_{jh}")
            for sub in range(nsub):
                s0 = sub * 128
                s1 = min(tw, s0 + 128)
                for jh in range(2):
                    P.mm(Prog.carve(pc[sub], 0, s1 - s0, jh * 512, (jh + 1) * 512, F32),
                         cx.ones_row[0:1, 0:s1 - s0], b2row[0:1, jh * 512:(jh + 1) * 512],
                         start=False, stop=True, tag=f"{tag}.yb{e}_{tgi}_{sub}")
                it = ga // 128 + sub
                gcol = G[0:s1 - s0, it * 8 + e:it * 8 + e + 1]
                yt = cx.c32(L1F["ytmp"] + (sub % 2) * 4096, 1024, p0=0, p1=s1 - s0)
                P.act(yt, Prog.carve(pc[sub], 0, s1 - s0, 0, 1024, F32), AF.Copy,
                      scale=gcol, tag=f"{tag}.ye{e}_{tgi}_{sub}")
                mv = moe_d.flat((ga + s0) * D, (ga + s1) * D)
                P.dma("pool", (mv.ap2d(s1 - s0, D), mv.region()), yt,
                      accum_op=(ALU.add if e > 0 else ALU.bypass),
                      tag=f"{tag}.ya{e}_{tgi}_{sub}")


G_TOK = G


import numpy as np


def _new_prog():
    nc = bass.Bass("TRN2", target_bir_lowering=False, debug=False)
    P = Prog(nc)
    cx = Ctx(P)
    return nc, P, cx


def _build_launch_A():
    nc, P, cx = _new_prog()
    x_d = P.dram("L0_attn_x", (2, 2048, 1024), F32, kind="ExternalInput")
    qkvw_d = P.dram("L0_attn_qkv_w", (3072, 1024), F32, kind="ExternalInput")
    qkvb_d = P.dram("L0_attn_qkv_b", (3072,), F32, kind="ExternalInput")
    projw_d = P.dram("L0_attn_proj_w", (1024, 1024), F32, kind="ExternalInput")
    projb_d = P.dram("L0_attn_proj_b", (1024,), F32, kind="ExternalInput")
    rel_d = P.dram("L0_attn_rel_emb", (16, 199, 64), F32, kind="ExternalInput")
    xs_d = P.dram("XS", (2, 2048, 1024), F32, kind="ExternalInput")
    xnscr_d = P.dram("XnS", (8 * 128 * 4096,), BF16)
    qscr_d = P.dram("QS", (8 * 128 * 4096,), BF16)
    xn1_d = P.dram("Xn1", (8 * 128 * 4096,), BF16)
    bo_d = P.dram("BO1", (4096 * 1024,), F32)
    r1_d = P.dram("R1", (4096 * 1024,), F32, kind="ExternalOutput")
    xn2_d = P.dram("Xn2", (8 * 128 * 4096,), BF16, kind="ExternalOutput")
    f_d = P.dram("F", (2048 * FPITCH + 2048,), F32)
    cx.setup()
    qkvw_bf = cast_cache(P, qkvw_d, "qkvw_bf")
    projw_bf = cast_cache(P, projw_d, "projw_bf")
    emit_boundary(P, cx, x_d, None, None, xn1_d, tag="b0")
    emit_boundary(P, cx, xs_d, None, None, xnscr_d, tag="bs")
    emit_qscr(P, cx, xnscr_d, qkvw_bf, qkvb_d, qscr_d, tag="qs")
    emit_l0_attn(P, cx, xn1_d, bo_d, qkvw_bf, projw_bf, rel_d, qkvb_d, projb_d,
                 f_d, qscr_d, tag="l0a")
    emit_boundary(P, cx, x_d, bo_d, r1_d, xn2_d, tag="b1")
    return P.materialize()


def _build_launch_B():
    nc, P, cx = _new_prog()
    xn2_d = P.dram("Xn2", (8 * 128 * 4096,), BF16, kind="ExternalInput")
    r1_d = P.dram("R1", (4096 * 1024,), F32, kind="ExternalInput")
    rm_d = P.dram("L0_ffn_rail_maker", (4, 64, 16, 16), F32, kind="ExternalInput")
    rup_d = P.dram("L0_ffn_rail_up_proj", (4, 64, 64, 16), F32, kind="ExternalInput")
    cw_d = P.dram("L0_ffn_w_cproj", (1024, 16384), F32, kind="ExternalInput")
    bo_d = P.dram("BO2", (4096 * 1024,), F32)
    r2_d = P.dram("R2", (4096 * 1024,), F32, kind="ExternalOutput")
    xn3_d = P.dram("Xn3", (8 * 128 * 4096,), BF16, kind="ExternalOutput")
    cx.setup()
    cw_bf = cast_cache(P, cw_d, "cw_bf")
    bdc = emit_l0f_bdc_image(P, cx, rm_d, rup_d)
    emit_l0_ffn(P, cx, xn2_d, bo_d, bdc, cw_bf, tag="l0f")
    emit_boundary(P, cx, r1_d, bo_d, r2_d, xn3_d, tag="b2")
    return P.materialize()


def _build_launch_C():
    nc, P, cx = _new_prog()
    xn3_d = P.dram("Xn3", (8 * 128 * 4096,), BF16, kind="ExternalInput")
    r2_d = P.dram("R2", (4096 * 1024,), F32, kind="ExternalInput")
    wds = {}
    shapes = {"Wq": (1024, 1024), "Wk": (1024, 1024), "Wv": (1024, 1024),
              "Wg1": (128, 1024), "Wg2": (1024, 128), "Wo": (1024, 1024)}
    for k, shp in shapes.items():
        wds[k] = P.dram(f"L1_attn_{k}", shp, F32, kind="ExternalInput")
    gnw_d = P.dram("L1_attn_g_norm_w", (1024,), F32, kind="ExternalInput")
    gam_d = P.dram("L1_ffn_gamma", (1024,), F32, kind="ExternalInput")
    bet_d = P.dram("L1_ffn_beta", (1024,), F32, kind="ExternalInput")
    bo_d = P.dram("BO3", (4096 * 1024,), F32)
    r3_d = P.dram("R3", (4096 * 1024,), F32, kind="ExternalOutput")
    xn4_d = P.dram("Xn4", (8 * 128 * 4096,), BF16, kind="ExternalOutput")
    st_d = P.dram("ST", (32 * 128 * 2,), F32, kind="ExternalOutput")
    cx.setup()
    wc = {k: cast_cache(P, wds[k], f"{k}_bf") for k in wds}
    ga_rep = make_rep_row(P, cx, gam_d, 0, 1024, "gam", "rg")
    be_rep = make_rep_row(P, cx, bet_d, 0, 1024, "bet", "rb")
    emit_l1_attn(P, cx, xn3_d, bo_d, wc, gnw_d, tag="l1a")
    emit_boundary(P, cx, r2_d, bo_d, r3_d, xn4_d, stats_out=st_d,
                  gamma_rep=ga_rep, beta_rep=be_rep, tag="b3")
    return P.materialize()


def _build_launch_D():
    nc, P, cx = _new_prog()
    xn4_d = P.dram("Xn4", (8 * 128 * 4096,), BF16, kind="ExternalInput")
    r3_d = P.dram("R3", (4096 * 1024,), F32, kind="ExternalInput")
    st_d = P.dram("ST", (32 * 128 * 2,), F32, kind="ExternalInput")
    wg_d = P.dram("L1_ffn_Wg", (8, 1024), F32, kind="ExternalInput")
    w1_d = P.dram("L1_ffn_W1", (8, 1024, 4096), F32, kind="ExternalInput")
    b1_d = P.dram("L1_ffn_b1", (8, 4096), F32, kind="ExternalInput")
    w2_d = P.dram("L1_ffn_W2", (8, 4096, 1024), F32, kind="ExternalInput")
    b2_d = P.dram("L1_ffn_b2", (8, 1024), F32, kind="ExternalInput")
    moe_d = P.dram("MOE", (4096 * 1024,), F32)
    out_d = P.dram("OUT", (2, 2048, 1024), F32, kind="ExternalOutput")
    cx.setup()
    emit_l1_ffn(P, cx, xn4_d, moe_d, wg_d, w1_d, b1_d, w2_d, b2_d, tag="l1f")
    emit_boundary(P, cx, r3_d, moe_d, out_d, None, prep_stats=st_d, tag="b4")
    return P.materialize()


def kernel(**inputs):
    ncores = 8
    x = np.ascontiguousarray(inputs["L0_attn_x"])
    shards = [np.ascontiguousarray(x[2 * c:2 * c + 2]) for c in range(ncores)]
    x_flat = x.reshape(16 * 2048, 1024)
    xs_shards = [np.ascontiguousarray(
        np.stack([x_flat[2 * c::16], x_flat[2 * c + 1::16]])) for c in range(ncores)]
    wk = {k: np.ascontiguousarray(v) for k, v in inputs.items() if k != "L0_attn_x"}

    ncA = _build_launch_A()
    imA = [{"L0_attn_x": shards[c], "XS": xs_shards[c],
            **{k: wk[k] for k in ["L0_attn_qkv_w", "L0_attn_qkv_b", "L0_attn_proj_w",
                                  "L0_attn_proj_b", "L0_attn_rel_emb"]}}
           for c in range(ncores)]
    rA = run_bass_kernel_spmd(ncA, imA, core_ids=list(range(ncores))).results

    ncB = _build_launch_B()
    imB = [{"Xn2": rA[c]["Xn2"], "R1": rA[c]["R1"],
            **{k: wk[k] for k in ["L0_ffn_rail_maker", "L0_ffn_rail_up_proj",
                                  "L0_ffn_w_cproj"]}} for c in range(ncores)]
    rB = run_bass_kernel_spmd(ncB, imB, core_ids=list(range(ncores))).results

    ncC = _build_launch_C()
    imC = [{"Xn3": rB[c]["Xn3"], "R2": rB[c]["R2"],
            **{k: wk[k] for k in ["L1_attn_Wq", "L1_attn_Wk", "L1_attn_Wv",
                                  "L1_attn_Wg1", "L1_attn_Wg2", "L1_attn_g_norm_w",
                                  "L1_attn_Wo", "L1_ffn_gamma", "L1_ffn_beta"]}}
           for c in range(ncores)]
    rC = run_bass_kernel_spmd(ncC, imC, core_ids=list(range(ncores))).results

    ncD = _build_launch_D()
    imD = [{"Xn4": rC[c]["Xn4"], "R3": rC[c]["R3"], "ST": rC[c]["ST"],
            **{k: wk[k] for k in ["L1_ffn_Wg", "L1_ffn_W1", "L1_ffn_b1",
                                  "L1_ffn_W2", "L1_ffn_b2"]}} for c in range(ncores)]
    rD = run_bass_kernel_spmd(ncD, imD, core_ids=list(range(ncores))).results

    out = np.concatenate([rD[c]["OUT"].reshape(2, 2048, 1024) for c in range(ncores)],
                         axis=0)
    return out


# version 09 (final)
# speedup vs baseline: 1.0386x; 1.0386x over previous
from __future__ import annotations
from concourse.bass_utils import run_bass_kernel_spmd
"""Mini raw-Bass emission framework with automatic semaphore insertion.

Two-phase: record ops (each = exactly one hardware instruction) with explicit
read/write regions, then materialize into per-engine instruction streams with
semaphore waits derived from a vector-clock dependency analysis.

Engines: 'pe' (TensorE), 'act' (ScalarE), 'dve' (VectorE), 'pool' (GpSimd),
'sp' (Sync).  DMA ops are attributed to the issuing engine ('sp'/'act'/'pool')
and complete on that queue's DMA semaphore (+16 per dma_start).
"""

import contextlib

import concourse.bass as bass
import concourse.mybir as mybir

F32 = mybir.dt.float32
BF16 = mybir.dt.bfloat16
AF = mybir.ActivationFunctionType
ALU = mybir.AluOpType
AX = mybir.AxisListType

ENGINES = ("pe", "act", "dve", "pool", "sp")
DMA_QUEUES = {"sp": "q_sp", "act": "q_act", "pool": "q_pool"}
ALL_SEMS = list(ENGINES) + list(DMA_QUEUES.values())
SEM_IDX = {name: i for i, name in enumerate(ALL_SEMS)}
ENG_IDX = {name: i for i, name in enumerate(ENGINES)}

SAME_ENGINE_SAFE = {"pe": True, "act": False, "dve": False, "pool": False, "sp": True}

FULL_SERIAL = False
DMA_SERIAL = True


class TV:

    __slots__ = ("h", "name", "space", "shape", "dtype", "p0", "p1", "f0", "f1")

    def __init__(self, h, name, space, shape, dtype, p0, p1, f0, f1):
        self.h = h
        self.name = name
        self.space = space
        self.shape = shape
        self.dtype = dtype
        self.p0, self.p1 = p0, p1
        self.f0, self.f1 = f0, f1

    def __getitem__(self, idx):
        assert self.space in ("sbuf", "psum")
        ps, fs = idx
        p0, p1 = _norm_slice(ps, self.p0, self.p1)
        f0, f1 = _norm_slice(fs, self.f0, self.f1)
        return TV(self.h, self.name, self.space, self.shape, self.dtype, p0, p1, f0, f1)

    def flat(self, e0, e1):
        assert self.space == "dram"
        assert self.p0 <= e0 <= e1 <= self.p1, (self.name, self.p0, self.p1, e0, e1)
        return TV(self.h, self.name, self.space, self.shape, self.dtype, e0, e1, 0, 0)

    def _flat_ap(self):
        a = self.h.ap()
        if len(self.shape) != 1:
            a = a.flatten()
        return a

    def ap(self):
        if self.space in ("sbuf", "psum"):
            a = self.h.ap()
            if a.dtype != self.dtype:
                a = a.bitcast(self.dtype)
            return a[self.p0:self.p1, self.f0:self.f1]
        return self._flat_ap()[self.p0:self.p1]

    def ap2d(self, rows, cols, row_pitch=None):
        assert self.space == "dram"
        pitch = cols if row_pitch is None else row_pitch
        a = self._flat_ap()
        import dataclasses
        return dataclasses.replace(a, offset=a.offset + self.p0,
                                   ap=[[pitch, rows], [1, cols]])

    def nd(self, *slices):
        assert self.space == "dram"
        ap = self.h.ap()[tuple(slices)]
        strides = []
        acc = 1
        for s in reversed(self.shape):
            strides.append(acc)
            acc *= s
        strides = strides[::-1]
        lo = 0
        hi = 0
        for i, dim in enumerate(self.shape):
            s = slices[i] if i < len(slices) else slice(None)
            if isinstance(s, int):
                a0, b0 = s, s + 1
            elif isinstance(s, slice):
                a0 = 0 if s.start is None else s.start
                b0 = dim if s.stop is None else s.stop
            else:
                raise TypeError(s)
            lo += a0 * strides[i]
            hi += (b0 - 1) * strides[i]
        return ap, (self.name, 0, 1, lo, hi + 1)

    def region(self):
        if self.space == "dram":
            return (self.name, 0, 1, self.p0, self.p1)
        b0 = self.f0 * _dtb(self.dtype)
        b1 = self.f1 * _dtb(self.dtype)
        if self.space == "psum":
            b0 = (b0 // 2048) * 2048
            b1 = ((b1 + 2047) // 2048) * 2048
        return (self.name, self.p0, self.p1, b0, b1)

    @property
    def pn(self):
        return self.p1 - self.p0

    @property
    def fn(self):
        return self.f1 - self.f0


def _norm_slice(s, lo, hi):
    if s is None or s == slice(None):
        return lo, hi
    if isinstance(s, slice):
        a = lo if s.start is None else lo + s.start
        b = hi if s.stop is None else lo + s.stop
        assert lo <= a <= b <= hi, (s, lo, hi)
        return a, b
    raise TypeError(s)


def _dtb(dt):
    return {F32: 4, BF16: 2, mybir.dt.float16: 2, mybir.dt.int32: 4,
            mybir.dt.uint8: 1, mybir.dt.uint16: 2, mybir.dt.int8: 1}[dt]


def regions_overlap(a, b):
    if a[0] != b[0]:
        return False
    if a[1] >= b[2] or b[1] >= a[2]:
        return False
    if a[3] >= b[4] or b[3] >= a[4]:
        return False
    return True


class Op:
    __slots__ = ("engine", "fn", "reads", "writes", "is_dma", "seq", "dseq",
                 "idx", "needs_inc", "deps", "tag")

    def __init__(self, engine, fn, reads, writes, is_dma, tag=""):
        self.engine = engine
        self.fn = fn
        self.reads = reads
        self.writes = writes
        self.is_dma = is_dma
        self.seq = 0
        self.dseq = 0
        self.idx = 0
        self.needs_inc = False
        self.deps = []
        self.tag = tag


class Prog:
    def __init__(self, nc: bass.Bass):
        self.nc = nc
        self.ops: list[Op] = []
        self.eng_count = {e: 0 for e in ENGINES}
        self.q_count = {q: 0 for q in DMA_QUEUES.values()}
        self._tensors = {}
        self._consts = {}
        self._dram_names = set()
        self.ctx = contextlib.ExitStack()
        self._sbuf_bytes = 0
        self.trace_tags = False

    def sbuf(self, name, p, f, dtype):
        h = self.ctx.enter_context(self.nc.sbuf_tensor(name, [p, f], dtype))
        self._sbuf_bytes += f * _dtb(dtype)
        return TV(h, name, "sbuf", (p, f), dtype, 0, p, 0, f)

    def psum(self, name, p, f, dtype=F32):
        h = self.ctx.enter_context(self.nc.psum_tensor(name, [p, f], dtype))
        return TV(h, name, "psum", (p, f), dtype, 0, p, 0, f)

    def dram(self, name, shape, dtype, kind="Internal"):
        h = self.nc.dram_tensor(name, list(shape), dtype, kind=kind)
        flatlen = 1
        for s in shape:
            flatlen *= s
        assert name not in self._dram_names
        self._dram_names.add(name)
        return TV(h, name, "dram", tuple(shape), dtype, 0, flatlen, 0, 0)

    def free(self, tv):
        pass

    @staticmethod
    def carve(arena, p0, p1, f0, f1, dtype=None):
        dt = arena.dtype if dtype is None else dtype
        return TV(arena.h, arena.name, arena.space, arena.shape, dt, p0, p1, f0, f1)

    def add(self, engine, fn, reads=(), writes=(), dma=False, tag=""):
        assert engine in ENGINES
        if dma:
            assert engine in DMA_QUEUES, f"engine {engine} cannot issue DMA"
        op = Op(engine, fn, [r.region() if isinstance(r, TV) else r for r in reads],
                [w.region() if isinstance(w, TV) else w for w in writes], dma, tag=tag)
        op.idx = len(self.ops)
        self.eng_count[engine] += 1
        op.seq = self.eng_count[engine]
        if dma:
            q = DMA_QUEUES[engine]
            self.q_count[q] += 1
            op.dseq = self.q_count[q]
        self.ops.append(op)
        return op

    def dma(self, queue, out, in_, tag="", **kw):
        oap, oreg = _apreg(out)
        iap, ireg = _apreg(in_)
        return self.add(queue, lambda e: e.dma_start(out=oap, in_=iap, **kw),
                        reads=[ireg], writes=[oreg], dma=True, tag=tag)

    def dma_transpose(self, queue, out, in_, tag=""):
        oap, oreg = _apreg(out)
        iap, ireg = _apreg(in_)
        return self.add(queue, lambda e: e.dma_start_transpose(out=oap, in_=iap),
                        reads=[ireg], writes=[oreg], dma=True, tag=tag)

    def mm(self, out, lhsT, rhs, start=True, stop=True, tag="", **kw):
        if isinstance(out, TV) and out.space == "psum":
            b0 = out.f0 * _dtb(out.dtype)
            b1 = out.f1 * _dtb(out.dtype) - 1
            assert b0 // 2048 == b1 // 2048, f"matmul output straddles PSUM banks: {out.name} {b0}..{b1}"
        oap, oreg = _apreg(out)
        lap, lreg = _apreg(lhsT)
        rap, rreg = _apreg(rhs)
        return self.add("pe", lambda e: e.matmul(oap, lap, rap, start=start, stop=stop, **kw),
                        reads=[lreg, rreg], writes=[oreg], tag=tag)

    def transpose(self, out, in_, ident, tag=""):
        oap, oreg = _apreg(out)
        iap, ireg = _apreg(in_)
        idap, idreg = _apreg(ident)
        return self.add("pe", lambda e: e.transpose(oap, iap, idap),
                        reads=[ireg, idreg], writes=[oreg], tag=tag)

    def const(self, value, dtype=F32):
        key = (float(value), dtype)
        if key not in self._consts:
            tv = self.sbuf(f"konst{len(self._consts)}", 128, 1, dtype)
            self.nc.const_aps.register(dtype, float(value), tv.ap())
            self.memset(tv, float(value), tag="konst")
            self._consts[key] = tv
        return self._consts[key]

    def act(self, out, in_, func, bias=0.0, scale=1.0, accum_out=None, tag=""):
        if func == AF.Copy and isinstance(bias, TV):
            func = AF.Identity
        oap, oreg = _apreg(out)
        iap, ireg = _apreg(in_)
        reads = [ireg]
        writes = [oreg]
        bias_ap, scale_ap = bias, scale
        if isinstance(bias, TV):
            bias_ap = bias.ap()
            reads.append(bias.region())
        elif isinstance(bias, float) and bias not in (0.0, 1.0):
            reads.append(self.const(bias).region())
        if isinstance(scale, TV):
            scale_ap = scale.ap()
            reads.append(scale.region())
        elif isinstance(scale, float) and scale not in (0.0, 1.0):
            reads.append(self.const(scale).region())
        acc_ap = None
        if accum_out is not None:
            acc_ap = accum_out.ap()
            writes.append(accum_out.region())
        return self.add("act",
                        lambda e: e.activation(oap, iap, func, bias=bias_ap,
                                               scale=scale_ap, accum_out=acc_ap),
                        reads=reads, writes=writes, tag=tag)

    def _veng(self, out, in_list, scalars, fn, tag, eng="dve"):
        oap, oreg = _apreg(out)
        reads = []
        aps = []
        for x in in_list:
            ap, reg = _apreg(x)
            aps.append(ap)
            reads.append(reg)
        sc = []
        for s in scalars:
            if isinstance(s, TV):
                sc.append(s.ap())
                reads.append(s.region())
            else:
                sc.append(s)
        return self.add(eng, lambda e: fn(e, oap, aps, sc), reads=reads,
                        writes=[oreg], tag=tag)

    def tt(self, out, a, b, op, tag=""):
        return self._veng(out, [a, b], [],
                          lambda e, o, i, s: e.tensor_tensor(o, i[0], i[1], op), tag)

    def ts(self, out, a, s1, s2=None, op0=ALU.mult, op1=ALU.bypass, tag=""):
        return self._veng(out, [a], [s1] + ([] if s2 is None else [s2]),
                          lambda e, o, i, s: e.tensor_scalar(
                              o, i[0], s[0], (s[1] if len(s) > 1 else None), op0, op1),
                          tag)

    def stt(self, out, in0, scalar, in1, op0, op1, tag=""):
        return self._veng(out, [in0, in1], [scalar],
                          lambda e, o, i, s: e.scalar_tensor_tensor(
                              o, i[0], s[0], i[1], op0, op1), tag)

    def scan(self, out, data0, data1, initial, op0, op1, tag=""):
        return self._veng(out, [data0, data1], [initial],
                          lambda e, o, i, s: e.tensor_tensor_scan(
                              o, i[0], i[1], s[0], op0, op1), tag)

    def reduce(self, out, in_, op=ALU.max, negate=None, tag=""):
        return self._veng(out, [in_], [],
                          lambda e, o, i, s: e.tensor_reduce(
                              o, i[0], AX.X, op, negate=negate), tag)

    def copy(self, out, in_, eng="dve", tag=""):
        return self._veng(out, [in_], [],
                          lambda e, o, i, s: e.tensor_copy(o, i[0]), tag, eng=eng)

    def recip(self, out, in_, tag=""):
        return self._veng(out, [in_], [],
                          lambda e, o, i, s: e.reciprocal(o, i[0]), tag)

    def memset(self, tv, value, eng="dve", tag=""):
        ap, reg = _apreg(tv)
        return self.add(eng, lambda e: e.memset(ap, value), reads=[], writes=[reg], tag=tag)

    def amemzero(self, tv, tag=""):
        ap, reg = _apreg(tv)
        return self.add("act", lambda e: e.memzero(ap), reads=[], writes=[reg], tag=tag)

    def afill(self, tv, value, tag=""):
        self.amemzero(tv, tag=f"{tag}z")
        return self.act(tv, tv, AF.Copy, bias=float(value), tag=f"{tag}f")

    def iota(self, tv, pattern, base=0, channel_multiplier=0, tag=""):
        ap, reg = _apreg(tv)
        return self.add("pool",
                        lambda e: e.iota(ap, pattern, base=base,
                                         channel_multiplier=channel_multiplier),
                        reads=[], writes=[reg], tag=tag)

    def affine_select(self, out, in_, pattern, compare_op, fill, base=0,
                      channel_multiplier=0, tag=""):
        oap, oreg = _apreg(out)
        iap, ireg = _apreg(in_)
        return self.add("pool",
                        lambda e: e.affine_select(oap, iap, pattern, compare_op, fill,
                                                  base=base,
                                                  channel_multiplier=channel_multiplier),
                        reads=[ireg], writes=[oreg], tag=tag)

    def select(self, out, mask, on_true, on_false, tag=""):
        return self._veng(out, [mask, on_true, on_false], [],
                          lambda e, o, i, s: e.select(o, i[0], i[1], i[2]), tag)

    def raw(self, engine, fn, reads=(), writes=(), tag=""):
        rd = [r.region() if isinstance(r, TV) else r for r in reads]
        wr = [w.region() if isinstance(w, TV) else w for w in writes]
        return self.add(engine, fn, rd, wr, tag=tag)

    def _resolve(self):
        writers: dict[str, list] = {}
        readers: dict[str, list] = {}
        nsem = len(ALL_SEMS)
        vc = {e: [0] * nsem for e in ENGINES}
        snapshots = [None] * len(self.ops)

        def comp_identity(op):
            if op.is_dma:
                return SEM_IDX[DMA_QUEUES[op.engine]], op.dseq
            return SEM_IDX[op.engine], op.seq

        for op in self.ops:
            deps = {}
            def consider(d):
                if d is op:
                    return
                sid, sseq = comp_identity(d)
                if d.engine == op.engine:
                    if (not d.is_dma) and (not op.is_dma) and SAME_ENGINE_SAFE[op.engine]:
                        return
                cur = deps.get(sid)
                if cur is None or sseq > cur[0]:
                    deps[sid] = (sseq, d)

            if FULL_SERIAL and op.idx > 0:
                consider(self.ops[op.idx - 1])
            elif DMA_SERIAL and op.idx > 0:
                prev = self.ops[op.idx - 1]
                if op.is_dma or prev.is_dma:
                    consider(prev)
            for r in op.reads:
                for (reg, d) in writers.get(r[0], ()):
                    if regions_overlap(r, reg):
                        consider(d)
            for w in op.writes:
                for (reg, d) in writers.get(w[0], ()):
                    if regions_overlap(w, reg):
                        consider(d)
                for (reg, d) in readers.get(w[0], ()):
                    if regions_overlap(w, reg):
                        consider(d)

            myvc = vc[op.engine]
            needed = []
            for sid, (sseq, d) in sorted(deps.items()):
                if myvc[sid] >= sseq:
                    continue
                needed.append((sid, d))
                d.needs_inc = True
                dsnap = snapshots[d.idx]
                if dsnap is not None:
                    for k in range(nsem):
                        if dsnap[k] > myvc[k]:
                            myvc[k] = dsnap[k]
                if sseq > myvc[sid]:
                    myvc[sid] = sseq
            op.deps = needed

            for r in op.reads:
                readers.setdefault(r[0], []).append((r, op))
                self._prune(readers[r[0]])
            for w in op.writes:
                writers.setdefault(w[0], []).append((w, op))
                self._prune(writers[w[0]])

            sid_own, sseq_own = comp_identity(op)
            snap = list(vc[op.engine])
            if sseq_own > snap[sid_own]:
                snap[sid_own] = sseq_own
            snapshots[op.idx] = snap

    @staticmethod
    def _prune(lst, cap=96):
        if len(lst) > cap:
            keep = lst[-cap // 2:]
            old = lst[:-cap // 2]
            name = old[0][0][0]
            by_identity = {}
            for reg, o in old:
                key = (o.engine, o.is_dma)
                cur = by_identity.get(key)
                if cur is None or o.idx > cur[1].idx:
                    by_identity[key] = (reg, o)
            union = (name, 0, 1 << 30, 0, 1 << 40)
            merged = [(union, o) for (_, o) in by_identity.values()]
            lst[:] = merged + keep

    def materialize(self, final_wait_engine="sp"):
        self._resolve()
        nc = self.nc
        inc_count = {}
        run = {e: 0 for e in ENGINES}
        for op in self.ops:
            if op.is_dma:
                op.needs_inc = True
        for op in self.ops:
            if op.is_dma:
                q = DMA_QUEUES[op.engine]
                inc_count[(SEM_IDX[q], op.dseq)] = op.dseq * 16
            elif op.needs_inc:
                run[op.engine] += 1
                inc_count[(SEM_IDX[op.engine], op.seq)] = run[op.engine]
        sems = {}
        per_engine_ops = {e: [] for e in ENGINES}
        for op in self.ops:
            per_engine_ops[op.engine].append(op)
        used = [e for e in ENGINES if per_engine_ops[e]]
        q_final = {q: self.q_count[q] * 16 for q in DMA_QUEUES.values() if self.q_count[q]}
        eng_final = {}
        for e in used:
            last = None
            for op in reversed(per_engine_ops[e]):
                if not op.is_dma:
                    last = op
                    break
            if last is not None and not last.needs_inc:
                last.needs_inc = True
                run[e] += 1
                inc_count[(SEM_IDX[e], last.seq)] = run[e]
            eng_final[e] = run[e]

        with contextlib.ExitStack() as stack:
            for s in ALL_SEMS:
                sems[s] = stack.enter_context(nc.semaphore(f"sem_{s}"))
            block = stack.enter_context(nc.Block())

            def emit_stream(eng_obj, eng_name):
                for op in per_engine_ops[eng_name]:
                    for sid, d in op.deps:
                        if d.is_dma:
                            val = d.dseq * 16
                        else:
                            val = inc_count[(sid, d.seq)]
                        eng_obj.wait_ge(sems[ALL_SEMS[sid]], val)
                    inst = op.fn(eng_obj)
                    if op.is_dma:
                        inst.then_inc(sems[DMA_QUEUES[op.engine]], 16)
                    elif op.needs_inc:
                        inst.then_inc(sems[op.engine], 1)
                if eng_name == final_wait_engine:
                    for q, v in q_final.items():
                        eng_obj.wait_ge(sems[q], v)
                    for e2 in used:
                        if e2 != eng_name and eng_final.get(e2, 0) > 0:
                            eng_obj.wait_ge(sems[e2], eng_final[e2])
                    for s2 in ALL_SEMS:
                        eng_obj.sem_clear(sems[s2])

            attr = {"pe": "tensor", "act": "scalar", "dve": "vector",
                    "pool": "gpsimd", "sp": "sync"}
            if final_wait_engine not in used:
                used.append(final_wait_engine)
            for e in used:
                getattr(block, attr[e])(
                    lambda eng_obj, _e=e: emit_stream(eng_obj, _e))
        self.ctx.close()
        return nc

    def stats(self):
        n_inc = sum(1 for op in self.ops if op.needs_inc)
        n_wait = sum(len(op.deps) for op in self.ops)
        per = {e: self.eng_count[e] for e in ENGINES if self.eng_count[e]}
        return dict(n_ops=len(self.ops), n_inc=n_inc, n_wait=n_wait, per_engine=per,
                    dma={q: n for q, n in self.q_count.items() if n})


def _apreg(x):
    if isinstance(x, TV):
        return x.ap(), x.region()
    if isinstance(x, tuple) and len(x) == 2:
        return x
    raise TypeError(x)


"""Block emitters for the mk_sandwich_d2 megakernel.

Per core: 2 batches x 2048 tokens x 1024 d. Global token g in [0,4096).

DRAM (f32 unless noted):
  R     flat 4096*1024, token-major.
  Xn    bf16 flat 8*128*4096, d-major: [k][p][g], d = 128k+p.
  stats flat 32*128*2 ([negmean, invstd] per token).
  BO    flat 4096*1024 (block raw output).
  F     flat 2048*FPITCH + 2048 (skew scratch).

Engines can't move data across partitions; all evacuations are lane-aligned.
"""

NEG = -1.0e8
FPITCH = 390
D = 1024
T = 2048
NB = 2
G = NB * T
NT = G // 128
EPS = 1e-5


def t_slices(total, step):
    return [(i, min(i + step, total)) for i in range(0, total, step)]


class Ctx:
    def __init__(self, P: Prog):
        self.P = P
        self.A = P.sbuf("A", 128, 86000, BF16)
        self.PSA = P.psum("PSA", 128, 1024, F32)
        self.PSB = P.psum("PSB", 128, 1024, F32)
        self.PSC = P.psum("PSC", 128, 1024, F32)
        self.PSD = P.psum("PSD", 128, 512, F32)
        self.PSE = P.psum("PSE", 128, 512, F32)
        self.ident = P.sbuf("ident", 128, 128, BF16)
        self.ident32 = P.sbuf("ident32", 128, 128, F32)
        self.ones_row = P.sbuf("ones_row", 1, 512, F32)
        self.cols = P.sbuf("cols", 128, 512, F32)
        self.col_n = 0

    def setup(self):
        P = self.P
        tmp16 = self.c16(0, 128)
        tmp16b = self.c16(256, 128)
        P.afill(tmp16, 1.0, tag="id_a")
        P.affine_select(tmp16b, tmp16, pattern=[[-1, 128]],
                        compare_op=ALU.is_equal, fill=0.0, base=0,
                        channel_multiplier=1, tag="ident")
        P.act(self.ident, tmp16b, AF.Copy, tag="id_c")
        tmp32 = self.c32(512, 128)
        tmp32b = self.c32(1024, 128)
        P.afill(tmp32, 1.0, tag="id32_a")
        P.affine_select(tmp32b, tmp32, pattern=[[-1, 128]],
                        compare_op=ALU.is_equal, fill=0.0, base=0,
                        channel_multiplier=1, tag="ident32")
        P.act(self.ident32, tmp32b, AF.Copy, tag="id32_c")
        P.afill(self.ones_row, 1.0, tag="ones")

    def c16(self, byte_off, width, p0=0, p1=128):
        assert byte_off % 2 == 0
        return Prog.carve(self.A, p0, p1, byte_off // 2, byte_off // 2 + width, BF16)

    def c32(self, byte_off, width, p0=0, p1=128):
        assert byte_off % 4 == 0
        return Prog.carve(self.A, p0, p1, byte_off // 4, byte_off // 4 + width, F32)

    def col(self, n=1):
        slot = (n + 15) & ~15
        if self.col_n + slot > 512:
            self.col_n = 0
        tv = Prog.carve(self.cols, 0, 128, self.col_n, self.col_n + n, F32)
        self.col_n += slot
        return tv

    def ps16(self, ps, p0, p1, f0_bytes, width):
        return Prog.carve(ps, p0, p1, f0_bytes // 2, f0_bytes // 2 + width, BF16)



def _ln_into(P, cx, x32, out32, scratch, add_to=None, tag=""):
    ssum = cx.col()
    ssq = cx.col()
    P.act(scratch, x32, AF.Square, accum_out=ssq, tag=f"{tag}sq")
    P.reduce(ssum, x32, op=ALU.add, tag=f"{tag}sm")
    stats = cx.col(2)
    negmean = stats[:, 0:1]
    invstd = stats[:, 1:2]
    P.ts(negmean, ssum, -1.0 / D, None, op0=ALU.mult, tag=f"{tag}nm")
    var = cx.col()
    msq = cx.col()
    P.tt(msq, negmean, negmean, ALU.mult, tag=f"{tag}mq")
    P.stt(var, ssq, 1.0 / D, msq, ALU.mult, ALU.subtract, tag=f"{tag}vr")
    std = cx.col()
    P.act(std, var, AF.Sqrt, bias=EPS, tag=f"{tag}sd")
    P.recip(invstd, std, tag=f"{tag}ri")
    if add_to is None:
        P.ts(out32, x32, negmean, invstd, op0=ALU.add, op1=ALU.mult, tag=f"{tag}n")
    else:
        P.ts(scratch, x32, negmean, invstd, op0=ALU.add, op1=ALU.mult, tag=f"{tag}n")
        P.tt(out32, scratch, add_to, ALU.add, tag=f"{tag}a")
    return stats


def emit_boundary(P, cx, src, bo, r_out, xn_out, stats_out=None,
                  gamma_rep=None, beta_rep=None, prep_stats=None, tag="bnd"):
    for it in range(NT):
        g0 = it * 128
        row0 = g0 * D
        base = (it % 2) * 24576
        srctile = cx.c32(base + 0, D)
        botile = cx.c32(base + 4096, D)
        rnew = cx.c32(base + 8192, D)
        xn32 = cx.c32(base + 12288, D)
        scratch = cx.c32(base + 16384, D)
        xn16 = cx.c16(base + 20480, D)
        xnT = cx.c16(base + 22528, D)

        sv = src.flat(row0, row0 + 128 * D)
        P.dma("sp", srctile, (sv.ap2d(128, D), sv.region()), tag=f"{tag}.s{it}")
        if bo is not None:
            bv = bo.flat(row0, row0 + 128 * D)
            P.dma("act", botile, (bv.ap2d(128, D), bv.region()), tag=f"{tag}.o{it}")
            if prep_stats is not None:
                stt_tile = cx.col(2)
                so = prep_stats.flat(it * 256, it * 256 + 256)
                P.dma("act", stt_tile, (so.ap2d(128, 2), so.region()),
                      tag=f"{tag}.ps{it}")
                xpl = cx.c32(base + 16384, D)
                P.ts(xpl, srctile, stt_tile[:, 0:1], stt_tile[:, 1:2],
                     op0=ALU.add, op1=ALU.mult, tag=f"{tag}.px{it}")
                P.tt(botile, botile, xpl, ALU.add, tag=f"{tag}.pa{it}")
            _ln_into(P, cx, botile, rnew, scratch, add_to=srctile, tag=f"{tag}.u{it}")
        else:
            rnew = srctile
        if r_out is not None:
            rv = r_out.flat(row0, row0 + 128 * D)
            P.dma("sp", (rv.ap2d(128, D), rv.region()), rnew, tag=f"{tag}.r{it}")
        if xn_out is None:
            continue
        stats = _ln_into(P, cx, rnew, xn32, scratch, tag=f"{tag}.v{it}")
        if stats_out is not None:
            so = stats_out.flat(it * 256, it * 256 + 256)
            P.dma("act", (so.ap2d(128, 2), so.region()), stats, tag=f"{tag}.t{it}")
        if gamma_rep is not None:
            P.tt(xn32, xn32, gamma_rep, ALU.mult, tag=f"{tag}.g{it}")
            P.tt(xn32, xn32, beta_rep, ALU.add, tag=f"{tag}.e{it}")
        P.act(xn16, xn32, AF.Copy, tag=f"{tag}.c{it}")
        for k in range(8):
            pst = cx.ps16(cx.PSD, 0, 128, 0, 128)
            P.transpose(pst, xn16[:, k * 128:(k + 1) * 128], cx.ident,
                        tag=f"{tag}.T{it}_{k}")
            P.act(xnT[:, k * 128:(k + 1) * 128], pst, AF.Copy, tag=f"{tag}.E{it}_{k}")
        dst_ap = xn_out._flat_ap()[0:8 * 128 * 4096].rearrange(
            "(k p g) -> p k g", k=8, p=128)[:, :, g0:g0 + 128]
        P.dma("sp", (dst_ap, xn_out.region()), xnT, tag=f"{tag}.x{it}")


def make_rep_row(P, cx, src_dram, e0, n, name, tag):
    row = P.sbuf(f"row_{name}", 1, n, F32)
    out32 = P.sbuf(f"rep_{name}", 128, n, F32)
    sv = src_dram.flat(e0, e0 + n)
    P.dma("sp", row, (sv.ap2d(1, n), sv.region()), tag=f"{tag}.ld")
    for a, b in t_slices(n, 512):
        P.mm(Prog.carve(cx.PSC, 0, 128, a, b, F32), cx.ones_row[0:1, 0:128],
             row[0:1, a:b], tag=f"{tag}.mm{a}")
    P.copy(out32, Prog.carve(cx.PSC, 0, 128, 0, n, F32), tag=f"{tag}.cp")
    return out32


def cast_cache(P, w_tv, name):
    n = w_tv.p1
    wbf = P.dram(name, (n,), BF16)
    CH = 1 << 21
    for a, b in t_slices(n, CH):
        P.dma("pool", wbf.flat(a, b), (w_tv._flat_ap()[a:b], (w_tv.name, 0, 1, a, b)),
              tag=f"cc.{name}.{a}")
    return wbf


def load_wT(P, cx, wbf, rows, cols, r0, rN, c0, out_off, tag, queue="sp"):
    out = cx.c16(out_off, rN)
    src = wbf.flat(0, rows * cols).ap2d(rows, cols)[r0:r0 + rN, c0:c0 + 128]
    P.dma_transpose(queue, out, (src, (wbf.name, 0, 1, r0 * cols, (r0 + rN) * cols)),
                    tag=tag)
    return out


def emit_qscr(P, cx, xnscr_dram, qkvw_bf, qkv_b, qscr_d, tag="qs"):
    qbs = P.sbuf(f"qbs_{tag}", 128, 8, F32)
    for j in range(8):
        sv = qkv_b.flat(j * 128, (j + 1) * 128)
        P.dma("act", qbs[:, j:j + 1], (sv.ap2d(128, 1), sv.region()),
              tag=f"{tag}.b{j}")
    for b_ in range(NB):
        g0 = b_ * T
        xn_sb = []
        for k in range(8):
            tvx = cx.c16(L0A["xn"] + k * 4096, T)
            src = xnscr_dram._flat_ap()[0:8 * 128 * 4096].rearrange(
                "(k p g) -> k p g", k=8, p=128)[k, :, g0:g0 + T]
            P.dma("sp", tvx, (src, xnscr_dram.region()), tag=f"{tag}.x{b_}_{k}")
            xn_sb.append(tvx)
        wT = [load_wT(P, cx, qkvw_bf, 3072, 1024, 0, 1024, k * 128,
                      L0A["wT"] + k * 2048, tag=f"{tag}.w{b_}_{k}")
              for k in range(8)]
        for i in range(8):
            qtile = cx.c16(L0A["qT"] + i * 4096, T)
            for j in range(4):
                ps = Prog.carve(cx.PSA if (i * 4 + j) % 2 == 0 else cx.PSB,
                                0, 128, 0, 512, F32)
                for k in range(8):
                    P.mm(ps, wT[k][:, i * 128:(i + 1) * 128],
                         xn_sb[k][:, j * 512:(j + 1) * 512],
                         start=(k == 0), stop=(k == 7),
                         tag=f"{tag}.m{b_}_{i}_{j}_{k}")
                P.act(qtile[:, j * 512:(j + 1) * 512], ps, AF.Copy,
                      bias=qbs[:, i:i + 1], tag=f"{tag}.e{b_}_{i}_{j}")
            dst = qscr_d._flat_ap()[0:8 * 128 * 4096].rearrange(
                "(i p g) -> i p g", i=8, p=128)[i, :, g0:g0 + T]
            P.dma("sp", (dst, qscr_d.region()), qtile, tag=f"{tag}.o{b_}_{i}")



L0A = dict(xn=0, qT=32768, kT=65536, v=98304, otok=0, wT=131072, scr=147456)


def emit_l0_attn(P, cx, xn_dram, bo, qkvw_bf, projw_bf, rel_f32, qkv_b, proj_b,
                 f_dram, qscr_d, prefill_F=True, tag="l0a"):
    H = 16
    qscr_sb = P.sbuf(f"qscr_{tag}", 128, 2048, BF16)

    qb = P.sbuf(f"qb_{tag}", 128, 16, F32)
    for j in range(16):
        sv = qkv_b.flat(j * 128, (j + 1) * 128)
        P.dma("act", qb[:, j:j + 1], (sv.ap2d(128, 1), sv.region()), tag=f"{tag}.qb{j}")
    vb_row = P.sbuf(f"vb_{tag}", 1, 1024, F32)
    sv = qkv_b.flat(2048, 3072)
    P.dma("act", vb_row, (sv.ap2d(1, 1024), sv.region()), tag=f"{tag}.vb")
    pb_row = P.sbuf(f"pb_{tag}", 1, 1024, F32)
    sv = proj_b.flat(0, 1024)
    P.dma("act", pb_row, (sv.ap2d(1, 1024), sv.region()), tag=f"{tag}.pb")

    relT = [P.sbuf(f"relT{h}_{tag}", 128, 199, BF16) for h in range(H)]
    for h in range(H):
        r32a = cx.c32(L0A["scr"], 64, p0=0, p1=128)
        r32b = cx.c32(L0A["scr"] + 256, 64, p0=0, p1=71)
        sv = rel_f32.flat(h * 199 * 64, h * 199 * 64 + 128 * 64)
        P.dma("sp", r32a, (sv.ap2d(128, 64), sv.region()), tag=f"{tag}.r{h}a")
        sv = rel_f32.flat(h * 199 * 64 + 128 * 64, (h + 1) * 199 * 64)
        P.dma("sp", r32b, (sv.ap2d(71, 64), sv.region()), tag=f"{tag}.r{h}b")
        r16a = cx.c16(L0A["scr"] + 512, 64, p0=0, p1=128)
        r16b = cx.c16(L0A["scr"] + 640, 64, p0=0, p1=71)
        P.act(r16a, r32a, AF.Copy, tag=f"{tag}.rc{h}a")
        P.act(r16b, r32b, AF.Copy, tag=f"{tag}.rc{h}b")
        psa = cx.ps16(cx.PSE, 0, 64, 0, 128)
        psb = cx.ps16(cx.PSE, 0, 64, 256, 71)
        P.transpose(psa, r16a, cx.ident, tag=f"{tag}.rT{h}a")
        P.transpose(psb, r16b, cx.ident[0:71, 0:71], tag=f"{tag}.rT{h}b")
        P.act(relT[h][0:64, 0:128], psa, AF.Copy, tag=f"{tag}.rE{h}a")
        P.act(relT[h][0:64, 128:199], psb, AF.Copy, tag=f"{tag}.rE{h}b")
        P.act(relT[h][64:128, 0:128], psa, AF.Copy, tag=f"{tag}.rE{h}c")
        P.act(relT[h][64:128, 128:199], psb, AF.Copy, tag=f"{tag}.rE{h}d")

    if prefill_F:
        negt = cx.c32(L0A["scr"] + 1024, FPITCH)
        P.afill(negt, NEG)
        fsize = 2048 * FPITCH + 2048
        nfull = fsize // (128 * FPITCH)
        for i in range(nfull):
            a = i * 128 * FPITCH
            fv = f_dram.flat(a, a + 128 * FPITCH)
            P.dma("sp", (fv.ap2d(128, FPITCH), fv.region()), negt[:, :],
                  tag=f"{tag}.fp{i}")
        a = nfull * 128 * FPITCH
        rem = fsize - a
        rows = rem // FPITCH
        if rows:
            fv = f_dram.flat(a, a + rows * FPITCH)
            P.dma("sp", (fv.ap2d(rows, FPITCH), fv.region()), negt[0:rows, :],
                  tag=f"{tag}.fpl")
        a += rows * FPITCH
        if fsize - a:
            fv = f_dram.flat(a, fsize)
            P.dma("sp", (fv.ap2d(1, fsize - a), fv.region()), negt[0:1, 0:fsize - a],
                  tag=f"{tag}.fpr")

    for b_ in range(NB):
        _l0a_batch(P, cx, b_, xn_dram, bo, qkvw_bf, projw_bf, relT, qb, vb_row,
                   pb_row, f_dram, qscr_d, qscr_sb, tag=f"{tag}.b{b_}")


def _l0a_batch(P, cx, b_, xn_dram, bo, qkvw_bf, projw_bf, relT, qb, vb_row,
               pb_row, f_dram, qscr_d, qscr_sb, tag):
    H = 16
    g0 = b_ * T
    xn_sb = []
    for k in range(8):
        tvx = cx.c16(L0A["xn"] + k * 4096, T)
        src = xn_dram._flat_ap()[0:8 * 128 * 4096].rearrange(
            "(k p g) -> k p g", k=8, p=128)[k, :, g0:g0 + T]
        P.dma("sp", tvx, (src, xn_dram.region()), tag=f"{tag}.xl{k}")
        xn_sb.append(tvx)

    qT = [cx.c16(L0A["qT"] + k * 4096, T) for k in range(8)]
    kTt = [cx.c16(L0A["kT"] + k * 4096, T) for k in range(8)]
    v_sb = [cx.c16(L0A["v"] + i * 2048, 1024) for i in range(16)]

    for proj in range(3):
        wT = [load_wT(P, cx, qkvw_bf, 3072, 1024, proj * 1024, 1024, k * 128,
                      L0A["wT"] + k * 2048, tag=f"{tag}.w{proj}_{k}")
              for k in range(8)]
        if proj in (0, 1):
            dst = qT if proj == 0 else kTt
            for i in range(8):
                for j in range(4):
                    ps = Prog.carve(cx.PSA if (i * 4 + j) % 2 == 0 else cx.PSB,
                                    0, 128, 0, 512, F32)
                    for k in range(8):
                        P.mm(ps, wT[k][:, i * 128:(i + 1) * 128],
                             xn_sb[k][:, j * 512:(j + 1) * 512],
                             start=(k == 0), stop=(k == 7),
                             tag=f"{tag}.m{proj}_{i}_{j}_{k}")
                    P.act(dst[i][:, j * 512:(j + 1) * 512], ps, AF.Copy,
                          bias=qb[:, proj * 8 + i:proj * 8 + i + 1],
                          tag=f"{tag}.e{proj}_{i}_{j}")
        else:
            for t_ in range(16):
                for jh in range(2):
                    ps = Prog.carve(cx.PSA if (t_ * 2 + jh) % 2 == 0 else cx.PSB,
                                    0, 128, 0, 512, F32)
                    for k in range(8):
                        P.mm(ps, xn_sb[k][:, t_ * 128:(t_ + 1) * 128],
                             wT[k][:, jh * 512:(jh + 1) * 512],
                             start=(k == 0), stop=False,
                             tag=f"{tag}.vm{t_}_{jh}_{k}")
                    P.mm(ps, cx.ones_row[0:1, 0:128],
                         vb_row[0:1, jh * 512:(jh + 1) * 512],
                         start=False, stop=True, tag=f"{tag}.vb{t_}_{jh}")
                    P.act(v_sb[t_][:, jh * 512:(jh + 1) * 512], ps, AF.Copy,
                          tag=f"{tag}.ve{t_}_{jh}")

    otok = [cx.c16(L0A["otok"] + i * 2048, 1024) for i in range(16)]
    pbuf = cx.c32(L0A["scr"] + 4096, 256)
    for h in range(H):
        qtv = qT[h // 2]
        pbase = 64 * (h % 2)
        if h % 2 == 0:
            qsrc = qscr_d._flat_ap()[0:8 * 128 * 4096].rearrange(
                "(i p g) -> i p g", i=8, p=128)[h // 2, :, b_ * T:(b_ + 1) * T]
            P.dma("sp", qscr_sb, (qsrc, qscr_d.region()), tag=f"{tag}.qs{h}")
        for t_ in range(16):
            ps = Prog.carve(cx.PSD, 0, 128, 0, 199, F32)
            P.mm(ps, qscr_sb[pbase:pbase + 64, t_ * 128:(t_ + 1) * 128],
                 relT[h][pbase:pbase + 64, :], tag=f"{tag}.P{h}_{t_}")
            P.act(pbuf[:, 0:199], ps, AF.Copy, tag=f"{tag}.Q{h}_{t_}")
            fv = f_dram.flat(t_ * 128 * FPITCH, (t_ * 128 + 128) * FPITCH)
            P.dma("sp", (fv.ap2d(128, 199, row_pitch=FPITCH), fv.region()),
                  pbuf[:, 0:199], tag=f"{tag}.W{h}_{t_}")
        for t_ in range(16):
            qs = t_ * 128
            ks = max(0, qs - 128)
            ke = min(T, qs + 256)
            Kt = ke - ks
            nch = Kt // 128
            psS = Prog.carve(cx.PSD, 0, 128, 0, Kt, F32)
            P.mm(psS, qtv[pbase:pbase + 64, qs:qs + 128],
                 kTt[h // 2][pbase:pbase + 64, ks:ke], tag=f"{tag}.S{h}_{t_}")
            ttile = cx.c32(L0A["scr"] + 8192, Kt)
            fbase = qs * (FPITCH - 1) + ks + 99
            fv = f_dram.flat(fbase, fbase + 128 * (FPITCH - 1))
            P.dma("act", ttile, (fv.ap2d(128, Kt, row_pitch=FPITCH - 1), fv.region()),
                  tag=f"{tag}.L{h}_{t_}")
            s2 = cx.c32(L0A["scr"] + 10240, Kt)
            P.stt(s2, psS, 0.125, ttile, ALU.mult, ALU.add, tag=f"{tag}.t{h}_{t_}")
            nmax = cx.col()
            P.reduce(nmax, s2, op=ALU.max, negate=True, tag=f"{tag}.x{h}_{t_}")
            pr = cx.c32(L0A["scr"] + 12288, Kt)
            ssum = cx.col()
            P.act(pr, s2, AF.Exp, bias=nmax, accum_out=ssum, tag=f"{tag}.a{h}_{t_}")
            rinv = cx.col()
            P.recip(rinv, ssum, tag=f"{tag}.i{h}_{t_}")
            pr16 = cx.c16(L0A["scr"] + 14336, Kt)
            P.act(pr16, pr, AF.Copy, scale=rinv, tag=f"{tag}.n{h}_{t_}")
            psT = cx.ps16(cx.PSE, 0, 128, 0, 384)
            for c in range(nch):
                P.transpose(psT[:, c * 128:(c + 1) * 128],
                            pr16[:, c * 128:(c + 1) * 128], cx.ident,
                            tag=f"{tag}.c{h}_{t_}_{c}")
            pt16 = cx.c16(L0A["scr"] + 16384, 384)
            P.act(pt16[:, 0:nch * 128], psT[:, 0:nch * 128], AF.Copy, tag=f"{tag}.d{h}_{t_}")
            psO = Prog.carve(cx.PSA, 0, 128, 0, 64, F32)
            for c in range(nch):
                kt_idx = ks // 128 + c
                P.mm(psO, pt16[:, c * 128:(c + 1) * 128],
                     v_sb[kt_idx][:, 64 * h:64 * h + 64],
                     start=(c == 0), stop=(c == nch - 1), tag=f"{tag}.v{h}_{t_}_{c}")
            P.act(otok[t_][:, 64 * h:64 * h + 64], psO, AF.Copy, tag=f"{tag}.o{h}_{t_}")

    pwT = [load_wT(P, cx, projw_bf, 1024, 1024, 0, 1024, k * 128,
                   L0A["wT"] + k * 2048, tag=f"{tag}.pw{k}") for k in range(8)]
    obuf = cx.c32(L0A["scr"] + 18432, D)
    oTt = cx.c16(L0A["scr"] + 8192, 128)
    for t_ in range(16):
        psP = Prog.carve(cx.PSB, 0, 128, 0, 1024, F32)
        first = True
        for k in range(8):
            pst = cx.ps16(cx.PSE, 0, 128, 0, 128)
            P.transpose(pst, otok[t_][:, k * 128:(k + 1) * 128], cx.ident,
                        tag=f"{tag}.f{t_}_{k}")
            P.act(oTt, pst, AF.Copy, tag=f"{tag}.g{t_}_{k}")
            for jh in range(2):
                P.mm(Prog.carve(cx.PSB, 0, 128, jh * 512, (jh + 1) * 512, F32),
                     oTt, pwT[k][:, jh * 512:(jh + 1) * 512],
                     start=(k == 0), stop=False, tag=f"{tag}.h{t_}_{k}_{jh}")
        for jh in range(2):
            P.mm(Prog.carve(cx.PSB, 0, 128, jh * 512, (jh + 1) * 512, F32),
                 cx.ones_row[0:1, 0:128], pb_row[0:1, jh * 512:(jh + 1) * 512],
                 start=False, stop=True, tag=f"{tag}.j{t_}_{jh}")
        P.act(obuf, psP, AF.Copy, tag=f"{tag}.k{t_}")
        bv = bo.flat((g0 + t_ * 128) * D, (g0 + t_ * 128 + 128) * D)
        P.dma("sp", (bv.ap2d(128, D), bv.region()), obuf, tag=f"{tag}.w{t_}")


L0F = dict(xn=0, bd1=8192, bd2=16384, yT=49152, hT=73728, cwT=75264, out=81920,
           stg=86016)


def emit_l0f_bdc_image(P, cx, rm_d, rup_d, tag="bdc"):
    bdc = P.dram("BDCi", (32 * 128 * 512,), BF16)
    z16 = cx.c16(L0F["stg"], 512)
    P.amemzero(z16)
    for hg in range(32):
        fv = bdc.flat(hg * 65536, (hg + 1) * 65536)
        P.dma("sp", (fv.ap2d(128, 512), fv.region()), z16[0:128, :], tag=f"{tag}.z{hg}")
    for h in range(4):
        for g in range(8):
            hg = h * 8 + g
            for a in range(8):
                r = 8 * g + a
                par = (a % 2) * 4096
                rm_s = cx.c32(L0F["stg"] + par + 1024, 16, p0=0, p1=16)
                rup_s = cx.c32(L0F["stg"] + par + 1088, 16, p0=0, p1=64)
                sv = rm_d.flat((h * 64 + r) * 256, (h * 64 + r) * 256 + 256)
                P.dma("pool", rm_s, (sv.ap2d(16, 16), sv.region()), tag=f"{tag}.m{hg}{a}")
                sv = rup_d.flat((h * 64 + r) * 1024, (h * 64 + r) * 1024 + 1024)
                P.dma("pool", rup_s, (sv.ap2d(64, 16), sv.region()), tag=f"{tag}.u{hg}{a}")
                ps1 = Prog.carve(cx.PSE, 0, 16, 0, 16, F32)
                P.transpose(ps1, rm_s, cx.ident32[0:16, 0:16], tag=f"{tag}.t1{hg}{a}")
                rmT = cx.c32(L0F["stg"] + par + 1536, 16, p0=0, p1=16)
                P.act(rmT, ps1, AF.Copy, tag=f"{tag}.e1{hg}{a}")
                ps2 = Prog.carve(cx.PSE, 0, 16, 64, 128, F32)
                P.transpose(ps2, rup_s, cx.ident32[0:64, 0:64], tag=f"{tag}.t2{hg}{a}")
                rupT = cx.c32(L0F["stg"] + par + 1600, 64, p0=0, p1=16)
                P.act(rupT, ps2, AF.Copy, tag=f"{tag}.e2{hg}{a}")
                psC = Prog.carve(cx.PSD, 0, 16, 0, 64, F32)
                P.mm(psC, rmT, rupT, tag=f"{tag}.c{hg}{a}")
                c16 = cx.c16(L0F["stg"] + par + 2048, 64, p0=0, p1=16)
                P.act(c16, psC, AF.Copy, tag=f"{tag}.ec{hg}{a}")
                base = hg * 65536
                dsta = bdc.flat(base + 16 * a * 512 + 64 * a,
                                base + (16 * a + 15) * 512 + 64 * a + 64)
                P.dma("sp", (dsta.ap2d(16, 64, row_pitch=512), dsta.region()),
                      c16, tag=f"{tag}.w{hg}{a}")
    return bdc


def emit_l0_ffn(P, cx, xn_dram, bo, bdc, cw_bf, tag="l0f"):
    bdc_sb = [cx.c16(L0F["bd2"] + hg * 1024, 512) for hg in range(32)]
    for hg in range(32):
        fv = bdc.flat(hg * 65536, (hg + 1) * 65536)
        P.dma("sp", bdc_sb[hg], (fv.ap2d(128, 512), fv.region()), tag=f"{tag}.lb{hg}")

    tgs = t_slices(G, 384)
    for tgi, (ga, gb) in enumerate(tgs):
        tw = gb - ga
        nsub = (tw + 127) // 128
        src = xn_dram._flat_ap()[0:8 * 128 * 4096].rearrange(
            "(k p g) -> p k g", k=8, p=128)[:, :, ga:gb]
        P.dma("sp", Prog.carve(cx.A, 0, 128, 0, 8 * tw, BF16),
              (src, xn_dram.region()), tag=f"{tag}.x{tgi}")
        xk = [Prog.carve(cx.A, 0, 128, k * tw, (k + 1) * tw, BF16) for k in range(8)]
        pc = [cx.PSA, cx.PSB, cx.PSC]
        for ft in range(128):
            hg, m = ft // 4, ft % 4
            cwT = cx.c16(L0F["cwT"] + (ft % 2) * 2048, 1024)
            src = cw_bf.flat(0, 1024 * 16384).ap2d(1024, 16384)[:, ft * 128:(ft + 1) * 128]
            P.dma_transpose("sp", cwT, (src, (cw_bf.name, 0, 1, 0, 1024 * 16384)),
                            tag=f"{tag}.cw_{tgi}_{ft}")
            psZ = Prog.carve(cx.PSE if ft % 2 == 0 else cx.PSD, 0, 128, 0, tw, F32)
            P.mm(psZ, bdc_sb[hg][:, m * 128:(m + 1) * 128], xk[hg % 8],
                 tag=f"{tag}.s2_{tgi}_{ft}")
            hT = cx.c16(L0F["hT"] + (ft % 2) * 768, tw)
            P.act(hT, psZ, AF.Gelu, tag=f"{tag}.g_{tgi}_{ft}")
            for sub in range(nsub):
                s0 = sub * 128
                s1 = min(tw, s0 + 128)
                for jh in range(2):
                    P.mm(Prog.carve(pc[sub], 0, 128, jh * 512, (jh + 1) * 512, F32),
                         hT[:, s0:s1], cwT[:, jh * 512:(jh + 1) * 512],
                         start=(ft == 0), stop=(ft == 127),
                         tag=f"{tag}.c_{tgi}_{ft}_{sub}_{jh}")
        for sub in range(nsub):
            s0 = sub * 128
            s1 = min(tw, s0 + 128)
            P.act(Prog.carve(cx.A, 0, s1 - s0, L0F["out"] // 4, L0F["out"] // 4 + D, F32),
                  Prog.carve(pc[sub], 0, s1 - s0, 0, D, F32), AF.Copy,
                  tag=f"{tag}.oe{tgi}_{sub}")
            bv = bo.flat((ga + s0) * D, (ga + s1) * D)
            P.dma("sp", (bv.ap2d(s1 - s0, D), bv.region()),
                  Prog.carve(cx.A, 0, s1 - s0, L0F["out"] // 4, L0F["out"] // 4 + D, F32),
                  tag=f"{tag}.ow{tgi}_{sub}")


L1A = dict(xn=0, wT=32768, e=49152, c=81920, tA=114688, tB=122880,
           q2=131072, g1=163840, scr=4096)


DBG = {}


def emit_l1_attn(P, cx, xn_dram, bo, wc, gnw_d, tag="l1a"):
    gn_rep = make_rep_row(P, cx, gnw_d, 0, 1024, f"gn_{tag}", f"{tag}.gn")
    ut_pre = cx.c16(L1A["scr"] + 16384, 64, p0=0, p1=64)
    ut_raw = cx.c16(L1A["scr"] + 16640, 64, p0=0, p1=64)
    ut = P.sbuf(f"ut_{tag}", 64, 64, F32)
    P.afill(ut_pre, 1.0, tag=f"{tag}.utp")
    P.affine_select(ut_raw, ut_pre, pattern=[[1, 64]], compare_op=ALU.is_ge,
                    fill=0.0, base=0, channel_multiplier=-1, tag=f"{tag}.uts")
    P.act(ut, ut_raw, AF.Copy, tag=f"{tag}.utc")
    shared = dict(
        cbs=P.sbuf(f"cb_{tag}", 128, 8 * 33, F32),
        rcbs=P.sbuf(f"rcb_{tag}", 128, 8 * 33, F32),
        S32=P.sbuf(f"S32_{tag}", 128, 8 * 128, F32),
        S16=P.sbuf(f"S16_{tag}", 128, 8 * 128, BF16),
    )
    for b_ in range(NB):
        _l1a_batch(P, cx, b_, xn_dram, bo, wc, gn_rep, ut, shared, tag=f"{tag}.b{b_}")


def _l1a_batch(P, cx, b_, xn_dram, bo, wc, gn_rep, ut, shared, tag):
    H = 8
    g0 = b_ * T
    v_dram = P.dram(f"vtmp_{tag}", (T * 1024,), BF16)
    xn_sb = []
    for k in range(8):
        tvx = cx.c16(L1A["xn"] + k * 4096, T)
        src = xn_dram._flat_ap()[0:8 * 128 * 4096].rearrange(
            "(k p g) -> k p g", k=8, p=128)[k, :, g0:g0 + T]
        P.dma("sp", tvx, (src, xn_dram.region()), tag=f"{tag}.xl{k}")
        xn_sb.append(tvx)

    e16 = [cx.c16(L1A["e"] + i * 4096, T) for i in range(8)]
    c16 = [cx.c16(L1A["c"] + i * 4096, T) for i in range(8)]
    q2 = [cx.c16(L1A["q2"] + i * 4096, T) for i in range(8)]
    g1 = cx.c16(L1A["g1"], T)
    cbs = shared["cbs"]
    rcbs = shared["rcbs"]

    sc1 = cx.c32(L1A["tB"], T)
    P.memset(sc1, 1.0, tag=f"{tag}.s1")

    wT = [load_wT(P, cx, wc["Wk"], 1024, 1024, 0, 1024, k * 128,
                  L1A["wT"] + k * 2048, tag=f"{tag}.wk{k}") for k in range(8)]
    for i in range(8):
        kr = cx.c32(L1A["tA"], T)
        for j in range(4):
            ps = Prog.carve(cx.PSA if j % 2 == 0 else cx.PSB, 0, 128, 0, 512, F32)
            for k in range(8):
                P.mm(ps, wT[k][:, i * 128:(i + 1) * 128],
                     xn_sb[k][:, j * 512:(j + 1) * 512],
                     start=(k == 0), stop=(k == 7), tag=f"{tag}.km{i}_{j}_{k}")
            P.copy(kr[:, j * 512:(j + 1) * 512], ps, tag=f"{tag}.ke{i}_{j}")
        nmax = cx.col()
        P.reduce(nmax, kr, op=ALU.max, negate=True, tag=f"{tag}.kx{i}")
        P.act(e16[i], kr, AF.Exp, bias=nmax, tag=f"{tag}.kp{i}")
        P.scan(c16[i], e16[i], sc1, 0.0, ALU.add, ALU.mult, tag=f"{tag}.kc{i}")
        P.copy(cbs[:, i * 33:i * 33 + 1], c16[i][:, 0:1], tag=f"{tag}.kb{i}")
        strided = cx_strided(c16[i], 63, 64, 32)
        P.raw("dve", lambda e, o=cbs[:, i * 33 + 1:i * 33 + 33].ap(), s=strided:
              e.tensor_copy(o, s), reads=[c16[i]],
              writes=[cbs[:, i * 33 + 1:i * 33 + 33]], tag=f"{tag}.kb2{i}")
        P.recip(rcbs[:, i * 33:(i + 1) * 33], cbs[:, i * 33:(i + 1) * 33],
                tag=f"{tag}.kr{i}")

    wT = [load_wT(P, cx, wc["Wq"], 1024, 1024, 0, 1024, k * 128,
                  L1A["wT"] + k * 2048, tag=f"{tag}.wq{k}") for k in range(8)]
    for i in range(8):
        qf = cx.c32(L1A["tA"] if i % 2 == 0 else L1A["tB"], T)
        for j in range(4):
            ps = Prog.carve(cx.PSA if j % 2 == 0 else cx.PSB, 0, 128, 0, 512, F32)
            for k in range(8):
                P.mm(ps, wT[k][:, i * 128:(i + 1) * 128],
                     xn_sb[k][:, j * 512:(j + 1) * 512],
                     start=(k == 0), stop=(k == 7), tag=f"{tag}.qm{i}_{j}_{k}")
            P.act(qf[:, j * 512:(j + 1) * 512], ps, AF.Silu, tag=f"{tag}.qs{i}_{j}")
        q2f = cx.c32(L1A["tB"] if i % 2 == 0 else L1A["tA"], T)
        for n in range(32):
            rct = cx.c32(L1A["scr"] + 43008, 64, p0=0, p1=128)
            P.recip(rct, c16[i][:, n * 64:(n + 1) * 64], tag=f"{tag}.qr{i}_{n}")
            cpcol = cbs[:, i * 33 + (0 if n == 0 else n):i * 33 + (0 if n == 0 else n) + 1]
            P.stt(q2f[:, n * 64:(n + 1) * 64], rct, cpcol,
                  qf[:, n * 64:(n + 1) * 64], ALU.mult, ALU.mult,
                  tag=f"{tag}.q2_{i}_{n}")
        P.act(q2[i], q2f, AF.Copy, tag=f"{tag}.qc{i}")

    wT = [load_wT(P, cx, wc["Wv"], 1024, 1024, 0, 1024, k * 128,
                  L1A["wT"] + k * 2048, tag=f"{tag}.wv{k}") for k in range(8)]
    for t_ in range(16):
        vtile = cx.c16(L1A["tA"] + (t_ % 2) * 8192, 1024)
        for jh in range(2):
            ps = Prog.carve(cx.PSA if (t_ + jh) % 2 == 0 else cx.PSB, 0, 128,
                            0, 512, F32)
            for k in range(8):
                P.mm(ps, xn_sb[k][:, t_ * 128:(t_ + 1) * 128],
                     wT[k][:, jh * 512:(jh + 1) * 512],
                     start=(k == 0), stop=(k == 7), tag=f"{tag}.vm{t_}_{jh}_{k}")
            P.act(vtile[:, jh * 512:(jh + 1) * 512], ps, AF.Copy,
                  tag=f"{tag}.vee{t_}_{jh}")
        dv = v_dram.flat(t_ * 128 * 1024, (t_ + 1) * 128 * 1024)
        P.dma("sp", (dv.ap2d(128, 1024), dv.region()), vtile, tag=f"{tag}.vw{t_}")

    wT1 = [load_wT(P, cx, wc["Wg1"], 128, 1024, 0, 128, k * 128,
                   L1A["wT"] + k * 2048, tag=f"{tag}.wg1{k}") for k in range(8)]
    for j in range(4):
        ps = Prog.carve(cx.PSA if j % 2 == 0 else cx.PSB, 0, 128, 0, 512, F32)
        for k in range(8):
            P.mm(ps, wT1[k], xn_sb[k][:, j * 512:(j + 1) * 512],
                 start=(k == 0), stop=(k == 7), tag=f"{tag}.gm{j}_{k}")
        P.act(g1[:, j * 512:(j + 1) * 512], ps, AF.Copy, tag=f"{tag}.ge{j}")

    woT = [load_wT(P, cx, wc["Wo"], 1024, 1024, 0, 1024, k * 128,
                   L1A["wT"] + k * 2048, tag=f"{tag}.wo{k}") for k in range(8)]
    wg2T = cx.c16(167936, 1024)
    src = wc["Wg2"].flat(0, 1024 * 128).ap2d(1024, 128)[:, 0:128]
    P.dma_transpose("sp", wg2T, (src, (wc["Wg2"].name, 0, 1, 0, 1024 * 128)),
                    tag=f"{tag}.wg2")

    S32 = shared["S32"]
    S16 = shared["S16"]
    P.amemzero(S32, tag=f"{tag}.sz")
    P.amemzero(S16, tag=f"{tag}.sz2")

    for n in range(32):
        par = n % 2
        vtok = Prog.carve(cx.A, 0, 64, (L1A["xn"] + par * 2048) // 2,
                          (L1A["xn"] + par * 2048) // 2 + 1024, BF16)
        dv = v_dram.flat(n * 64 * 1024, (n + 1) * 64 * 1024)
        P.dma("act", vtok, (dv.ap2d(64, 1024), dv.region()), tag=f"{tag}.lv{n}")
        psO = Prog.carve(cx.PSA, 0, 64, 0, 1024, F32)
        for h in range(8):
            cp = cbs[:, h * 33 + (0 if n == 0 else n):h * 33 + (0 if n == 0 else n) + 1]
            rcp = rcbs[:, h * 33 + (0 if n == 0 else n):h * 33 + (0 if n == 0 else n) + 1]
            rce = rcbs[:, h * 33 + n + 1:h * 33 + n + 2]
            ech = e16[h][:, n * 64:(n + 1) * 64]
            k2t = cx.c16(L1A["scr"] + par * 16384, 64)
            P.act(k2t, ech, AF.Copy, scale=rcp, tag=f"{tag}.k2_{n}_{h}")
            psA = Prog.carve(cx.PSD, 0, 64, 0, 64, F32)
            P.mm(psA, k2t, q2[h][:, n * 64:(n + 1) * 64], tag=f"{tag}.am_{n}_{h}")
            atf = cx.c32(L1A["scr"] + par * 16384 + 256, 64, p0=0, p1=64)
            P.tt(atf, psA, ut, ALU.mult, tag=f"{tag}.ms_{n}_{h}")
            at16 = cx.c16(L1A["scr"] + par * 16384 + 1024, 64, p0=0, p1=64)
            P.act(at16, atf, AF.Copy, tag=f"{tag}.ac_{n}_{h}")
            ktt = cx.c16(L1A["scr"] + par * 16384 + 1280, 64)
            P.act(ktt, ech, AF.Copy, scale=rce, tag=f"{tag}.kt_{n}_{h}")
            psKT = cx.ps16(cx.PSD, 0, 64, 512, 128)
            P.transpose(psKT, ktt, cx.ident[0:128, 0:128], tag=f"{tag}.ktT_{n}_{h}")
            kttok = cx.c16(L1A["scr"] + par * 16384 + 1536, 128, p0=0, p1=64)
            P.act(kttok, psKT, AF.Copy, tag=f"{tag}.kte_{n}_{h}")
            if n == 0 and h == 0:
                DBG["at16"] = at16
                DBG["kttok"] = kttok
            P.mm(psO[:, h * 128:(h + 1) * 128], q2[h][:, n * 64:(n + 1) * 64],
                 S16[:, h * 128:(h + 1) * 128], start=True, stop=False,
                 tag=f"{tag}.oi_{n}_{h}")
            P.mm(psO[:, h * 128:(h + 1) * 128], at16,
                 vtok[:, h * 128:(h + 1) * 128], start=False, stop=True,
                 tag=f"{tag}.oo_{n}_{h}")
            psU = Prog.carve(cx.PSB, 0, 128, h * 128, (h + 1) * 128, F32)
            P.mm(psU, kttok, vtok[:, h * 128:(h + 1) * 128], tag=f"{tag}.su_{n}_{h}")
            decay = cx.col()
            P.tt(decay, cp, rce, ALU.mult, tag=f"{tag}.dc_{n}_{h}")
            P.stt(S32[:, h * 128:(h + 1) * 128], S32[:, h * 128:(h + 1) * 128],
                  decay, psU, ALU.mult, ALU.add, tag=f"{tag}.sn_{n}_{h}")
            P.act(S16[:, h * 128:(h + 1) * 128], S32[:, h * 128:(h + 1) * 128],
                  AF.Copy, tag=f"{tag}.sc_{n}_{h}")
        if DBG.get("d") is not None and n == 0 and "b1" in tag:
            dbg = DBG["d"]
            stg2 = cx.c32(L1A["tB"] + 4096, 1024, p0=0, p1=64)
            P.copy(stg2, psO, tag="dbg_psO_cp")
            dv = dbg.flat(0, 64 * 1024)
            P.dma("sp", (dv.ap2d(64, 1024), dv.region()), stg2, tag="dbg_psO")
            P.act(stg2[0:64, 0:128], S16[0:64, 0:128], AF.Copy, tag="dbg_s16c")
            dv = dbg.flat(64 * 1024, 64 * 1024 + 64 * 128)
            P.dma("sp", (dv.ap2d(64, 128), dv.region()), stg2[0:64, 0:128], tag="dbg_s16")
            P.act(stg2[0:64, 0:64], DBG["at16"], AF.Copy, tag="dbg_atc")
            dv = dbg.flat(64 * 1024 + 64 * 128, 64 * 1024 + 64 * 128 + 64 * 64)
            P.dma("sp", (dv.ap2d(64, 64), dv.region()), stg2[0:64, 0:64], tag="dbg_at")
            P.act(stg2[0:64, 0:128], DBG["kttok"], AF.Copy, tag="dbg_ktc")
            dv = dbg.flat(64 * 1024 + 64 * 128 + 64 * 64,
                          64 * 1024 + 64 * 128 + 64 * 64 + 64 * 128)
            P.dma("sp", (dv.ap2d(64, 128), dv.region()), stg2[0:64, 0:128], tag="dbg_kt")
        osq = cx.c32(L1A["tA"], 1024, p0=0, p1=64)
        ssq = cx.col()[0:64, :]
        P.act(osq, psO, AF.Square, accum_out=ssq, tag=f"{tag}.rq{n}")
        rstd = cx.col()[0:64, :]
        sq = cx.col()[0:64, :]
        P.ts(sq, ssq, 1.0 / 1024.0, None, op0=ALU.mult, tag=f"{tag}.rm{n}")
        sqs = cx.col()[0:64, :]
        P.act(sqs, sq, AF.Sqrt, bias=EPS, tag=f"{tag}.rs{n}")
        P.recip(rstd, sqs, tag=f"{tag}.rr{n}")
        o_n = cx.c32(L1A["tA"] + 4096, 1024, p0=0, p1=64)
        P.stt(o_n, psO, rstd, gn_rep[0:64, :], ALU.mult, ALU.mult,
              tag=f"{tag}.rn{n}")
        psG = Prog.carve(cx.PSC, 0, 64, 0, 1024, F32)
        for jh in range(2):
            P.mm(Prog.carve(cx.PSC, 0, 64, jh * 512, (jh + 1) * 512, F32),
                 g1[:, n * 64:(n + 1) * 64], wg2T[:, jh * 512:(jh + 1) * 512],
                 tag=f"{tag}.g2_{n}_{jh}")
        g2s = cx.c32(L1A["tB"], 1024, p0=0, p1=64)
        P.act(g2s, psG, AF.Silu, tag=f"{tag}.gs{n}")
        rs32 = cx.c32(L1A["tB"] + 4096, 1024, p0=0, p1=64)
        P.tt(rs32, o_n, g2s, ALU.mult, tag=f"{tag}.rx{n}")
        rs16 = cx.c16(L1A["scr"] + 20480, 1024, p0=0, p1=64)
        P.act(rs16, rs32, AF.Copy, tag=f"{tag}.rc{n}")
        psW = Prog.carve(cx.PSC, 0, 64, 0, 1024, F32)
        for k in range(8):
            psT = cx.ps16(cx.PSD, 0, 128, 1024, 64)
            P.transpose(psT, rs16[:, k * 128:(k + 1) * 128], cx.ident[0:64, 0:64],
                        tag=f"{tag}.wt{n}_{k}")
            rsT = cx.c16(L1A["scr"] + 22528, 64)
            P.act(rsT, psT, AF.Copy, tag=f"{tag}.we{n}_{k}")
            for jh in range(2):
                P.mm(Prog.carve(cx.PSC, 0, 64, jh * 512, (jh + 1) * 512, F32),
                     rsT, woT[k][:, jh * 512:(jh + 1) * 512],
                     start=(k == 0), stop=(k == 7), tag=f"{tag}.wm{n}_{k}_{jh}")
        ob = cx.c32(L1A["tA"], 1024, p0=0, p1=64)
        P.act(ob, psW, AF.Copy, tag=f"{tag}.ob{n}")
        if DBG.get("d2") is not None and n == 0 and "b1" in tag:
            d2 = DBG["d2"]
            wdump = cx.c32(L1A["tB"], 1024, p0=0, p1=128)
            P.copy(wdump, woT[0], tag="dbgw_cp")
            dv = DBG["d3"].flat(0, 128 * 1024)
            P.dma("sp", (dv.ap2d(128, 1024), dv.region()), wdump, tag="dbgw")
            P.copy(wdump[:, 0:64], rsT, tag="dbgr_cp")
            dv = DBG["d3"].flat(128 * 1024, 128 * 1024 + 128 * 64)
            P.dma("sp", (dv.ap2d(128, 64), dv.region()), wdump[:, 0:64], tag="dbgr")
            for si, tvv in enumerate([o_n, g2s, rs32, ob]):
                dv = d2.flat(si * 65536, (si + 1) * 65536)
                P.dma("sp", (dv.ap2d(64, 1024), dv.region()), tvv, tag=f"dbg2_{si}")
        bv = bo.flat((g0 + n * 64) * D, (g0 + (n + 1) * 64) * D)
        P.dma("sp", (bv.ap2d(64, D), bv.region()), ob, tag=f"{tag}.bw{n}")


def cx_strided(tv, start, step, count):
    import dataclasses
    a = tv.ap()
    return dataclasses.replace(a, offset=a.offset + start,
                               ap=[[a.ap[0][0], a.ap[0][1]], [step, count]])


L1F = dict(w1=0, w2=65536, xn=131072, hT=143360, ytmp=145408, scr=153600)


def emit_l1_ffn(P, cx, xn_dram, moe_d, wg_d, w1_d, b1_d, w2_d, b2_d, tag="l1f"):
    wg_sb = cx.c32(L1F["scr"], 1024, p0=0, p1=8)
    sv = wg_d.flat(0, 8 * 1024)
    P.dma("sp", wg_sb, (sv.ap2d(8, 1024), sv.region()), tag=f"{tag}.wgl")
    wg16 = cx.c16(L1F["scr"] + 4096, 1024, p0=0, p1=8)
    P.act(wg16, wg_sb, AF.Copy, tag=f"{tag}.wgc")
    wgT = [cx.c16(L1F["scr"] + 6144 + k * 16, 8) for k in range(8)]
    for k in range(8):
        pst = cx.ps16(cx.PSD, 0, 128, 0, 8)
        P.transpose(pst, wg16[0:8, k * 128:(k + 1) * 128], cx.ident[0:8, 0:8],
                    tag=f"{tag}.wgT{k}")
        P.act(wgT[k], pst, AF.Copy, tag=f"{tag}.wgE{k}")
    G = P.sbuf(f"G_{tag}", 128, NT * 8, F32)
    for it in range(NT):
        g0 = it * 128
        xsl = cx.c16(L1F["scr"] + 8192 + (it % 2) * 2048, 8 * 128)
        src = xn_dram._flat_ap()[0:8 * 128 * 4096].rearrange(
            "(k p g) -> p k g", k=8, p=128)[:, :, g0:g0 + 128]
        P.dma("sp", xsl, (src, xn_dram.region()), tag=f"{tag}.gx{it}")
        psg = Prog.carve(cx.PSE, 0, 128, 0, 8, F32)
        for k in range(8):
            P.mm(psg, Prog.carve(cx.A, 0, 128,
                                 (L1F["scr"] + 8192 + (it % 2) * 2048) // 2 + k * 128,
                                 (L1F["scr"] + 8192 + (it % 2) * 2048) // 2 + (k + 1) * 128,
                                 BF16),
                 wgT[k], start=(k == 0), stop=(k == 7), tag=f"{tag}.gm{it}_{k}")
        lg = cx.c32(L1F["scr"] + 12288, 8)
        P.copy(lg, psg, tag=f"{tag}.ge{it}")
        nmx = cx.col()
        P.reduce(nmx, lg, op=ALU.max, negate=True, tag=f"{tag}.gq{it}")
        pr = cx.c32(L1F["scr"] + 12320, 8)
        ssum = cx.col()
        P.act(pr, lg, AF.Exp, bias=nmx, accum_out=ssum, tag=f"{tag}.gp{it}")
        rs = cx.col()
        P.recip(rs, ssum, tag=f"{tag}.gr{it}")
        probs = cx.c32(L1F["scr"] + 12352, 8)
        P.ts(probs, pr, rs, None, op0=ALU.mult, tag=f"{tag}.gn{it}")
        m1 = cx.col()
        P.reduce(m1, probs, op=ALU.max, tag=f"{tag}.g1{it}")
        is1 = cx.c32(L1F["scr"] + 12384, 8)
        P.ts(is1, probs, m1, None, op0=ALU.is_ge, tag=f"{tag}.gi{it}")
        p2 = cx.c32(L1F["scr"] + 12416, 8)
        P.stt(p2, is1, -1e30, probs, ALU.mult, ALU.add, tag=f"{tag}.g2{it}")
        m2 = cx.col()
        P.reduce(m2, p2, op=ALU.max, tag=f"{tag}.g3{it}")
        den = cx.col()
        P.tt(den, m1, m2, ALU.add, tag=f"{tag}.g4{it}")
        rden = cx.col()
        dn2 = cx.col()
        P.ts(dn2, den, 1e-9, None, op0=ALU.add, tag=f"{tag}.g5{it}")
        P.recip(rden, dn2, tag=f"{tag}.g6{it}")
        mask = cx.c32(L1F["scr"] + 12448, 8)
        P.ts(mask, probs, m2, None, op0=ALU.is_ge, tag=f"{tag}.g7{it}")
        gm = cx.c32(L1F["scr"] + 12480, 8)
        P.tt(gm, probs, mask, ALU.mult, tag=f"{tag}.g8{it}")
        P.ts(G[:, it * 8:(it + 1) * 8], gm, rden, None, op0=ALU.mult,
             tag=f"{tag}.g9{it}")

    b1sb = P.sbuf(f"b1_{tag}", 128, 32, F32)
    b2row = P.sbuf(f"b2_{tag}", 1, 1024, F32)
    w1_sb = [cx.c16(L1F["w1"] + k * 8192, 4096) for k in range(8)]
    w2_sb = [cx.c16(L1F["w2"] + k * 2048, 1024) for k in range(32)]
    tgs = t_slices(G_TOK, 384)
    pc = [cx.PSA, cx.PSB, cx.PSC]
    for e in range(8):
        for k in range(8):
            ap, reg = w1_d.nd(e, slice(k * 128, (k + 1) * 128), slice(None))
            P.dma("pool", w1_sb[k], (ap, reg), tag=f"{tag}.w1_{e}_{k}")
        for k in range(32):
            ap, reg = w2_d.nd(e, slice(k * 128, (k + 1) * 128), slice(None))
            P.dma("pool", w2_sb[k], (ap, reg), tag=f"{tag}.w2_{e}_{k}")
        for j in range(32):
            sv = b1_d.flat(e * 4096 + j * 128, e * 4096 + (j + 1) * 128)
            P.dma("act", b1sb[:, j:j + 1], (sv.ap2d(128, 1), sv.region()),
                  tag=f"{tag}.b1_{e}_{j}")
        sv = b2_d.flat(e * 1024, (e + 1) * 1024)
        P.dma("act", b2row, (sv.ap2d(1, 1024), sv.region()), tag=f"{tag}.b2_{e}")
        for tgi, (ga, gb) in enumerate(tgs):
            tw = gb - ga
            nsub = (tw + 127) // 128
            xbase = L1F["xn"] + (tgi % 2) * 6144
            src = xn_dram._flat_ap()[0:8 * 128 * 4096].rearrange(
                "(k p g) -> p k g", k=8, p=128)[:, :, ga:gb]
            P.dma("sp", Prog.carve(cx.A, 0, 128, xbase // 2, xbase // 2 + 8 * tw, BF16),
                  (src, xn_dram.region()), tag=f"{tag}.x{e}_{tgi}")
            xk = [Prog.carve(cx.A, 0, 128, xbase // 2 + k * tw,
                             xbase // 2 + (k + 1) * tw, BF16) for k in range(8)]
            for ft in range(32):
                psH = Prog.carve(cx.PSD if ft % 2 == 0 else cx.PSE, 0, 128, 0, tw, F32)
                for k in range(8):
                    P.mm(psH, w1_sb[k][:, ft * 128:(ft + 1) * 128], xk[k],
                         start=(k == 0), stop=(k == 7), tag=f"{tag}.h{e}_{tgi}_{ft}_{k}")
                hT = cx.c16(L1F["hT"] + (ft % 2) * 768, tw)
                P.act(hT, psH, AF.Gelu_apprx_tanh, bias=b1sb[:, ft:ft + 1],
                      tag=f"{tag}.g{e}_{tgi}_{ft}")
                for sub in range(nsub):
                    s0 = sub * 128
                    s1 = min(tw, s0 + 128)
                    for jh in range(2):
                        P.mm(Prog.carve(pc[sub], 0, s1 - s0, jh * 512, (jh + 1) * 512, F32),
                             hT[:, s0:s1], w2_sb[ft][:, jh * 512:(jh + 1) * 512],
                             start=(ft == 0), stop=False,
                             tag=f"{tag}.y{e}_{tgi}_{ft}_{sub}_{jh}")
            for sub in range(nsub):
                s0 = sub * 128
                s1 = min(tw, s0 + 128)
                for jh in range(2):
                    P.mm(Prog.carve(pc[sub], 0, s1 - s0, jh * 512, (jh + 1) * 512, F32),
                         cx.ones_row[0:1, 0:s1 - s0], b2row[0:1, jh * 512:(jh + 1) * 512],
                         start=False, stop=True, tag=f"{tag}.yb{e}_{tgi}_{sub}")
                it = ga // 128 + sub
                gcol = G[0:s1 - s0, it * 8 + e:it * 8 + e + 1]
                yt = cx.c32(L1F["ytmp"] + (sub % 2) * 4096, 1024, p0=0, p1=s1 - s0)
                P.act(yt, Prog.carve(pc[sub], 0, s1 - s0, 0, 1024, F32), AF.Copy,
                      scale=gcol, tag=f"{tag}.ye{e}_{tgi}_{sub}")
                mv = moe_d.flat((ga + s0) * D, (ga + s1) * D)
                P.dma("pool", (mv.ap2d(s1 - s0, D), mv.region()), yt,
                      accum_op=(ALU.add if e > 0 else ALU.bypass),
                      tag=f"{tag}.ya{e}_{tgi}_{sub}")


G_TOK = G


import numpy as np


def _new_prog():
    nc = bass.Bass("TRN2", target_bir_lowering=False, debug=False)
    P = Prog(nc)
    cx = Ctx(P)
    return nc, P, cx


def _build_launch_A():
    nc, P, cx = _new_prog()
    x_d = P.dram("L0_attn_x", (2, 2048, 1024), F32, kind="ExternalInput")
    qkvw_d = P.dram("L0_attn_qkv_w", (3072, 1024), F32, kind="ExternalInput")
    qkvb_d = P.dram("L0_attn_qkv_b", (3072,), F32, kind="ExternalInput")
    projw_d = P.dram("L0_attn_proj_w", (1024, 1024), F32, kind="ExternalInput")
    projb_d = P.dram("L0_attn_proj_b", (1024,), F32, kind="ExternalInput")
    rel_d = P.dram("L0_attn_rel_emb", (16, 199, 64), F32, kind="ExternalInput")
    xs_d = P.dram("XS", (2, 2048, 1024), F32, kind="ExternalInput")
    xnscr_d = P.dram("XnS", (8 * 128 * 4096,), BF16)
    qscr_d = P.dram("QS", (8 * 128 * 4096,), BF16)
    xn1_d = P.dram("Xn1", (8 * 128 * 4096,), BF16)
    bo_d = P.dram("BO1", (4096 * 1024,), F32)
    r1_d = P.dram("R1", (4096 * 1024,), F32, kind="ExternalOutput")
    xn2_d = P.dram("Xn2", (8 * 128 * 4096,), BF16, kind="ExternalOutput")
    f_d = P.dram("F", (2048 * FPITCH + 2048,), F32)
    cx.setup()
    qkvw_bf = cast_cache(P, qkvw_d, "qkvw_bf")
    projw_bf = cast_cache(P, projw_d, "projw_bf")
    emit_boundary(P, cx, x_d, None, None, xn1_d, tag="b0")
    emit_boundary(P, cx, xs_d, None, None, xnscr_d, tag="bs")
    emit_qscr(P, cx, xnscr_d, qkvw_bf, qkvb_d, qscr_d, tag="qs")
    emit_l0_attn(P, cx, xn1_d, bo_d, qkvw_bf, projw_bf, rel_d, qkvb_d, projb_d,
                 f_d, qscr_d, tag="l0a")
    emit_boundary(P, cx, x_d, bo_d, r1_d, xn2_d, tag="b1")
    return P.materialize()


def _build_launch_B():
    nc, P, cx = _new_prog()
    xn2_d = P.dram("Xn2", (8 * 128 * 4096,), BF16, kind="ExternalInput")
    r1_d = P.dram("R1", (4096 * 1024,), F32, kind="ExternalInput")
    rm_d = P.dram("L0_ffn_rail_maker", (4, 64, 16, 16), F32, kind="ExternalInput")
    rup_d = P.dram("L0_ffn_rail_up_proj", (4, 64, 64, 16), F32, kind="ExternalInput")
    cw_d = P.dram("L0_ffn_w_cproj", (1024, 16384), F32, kind="ExternalInput")
    bo_d = P.dram("BO2", (4096 * 1024,), F32)
    r2_d = P.dram("R2", (4096 * 1024,), F32, kind="ExternalOutput")
    xn3_d = P.dram("Xn3", (8 * 128 * 4096,), BF16, kind="ExternalOutput")
    cx.setup()
    cw_bf = cast_cache(P, cw_d, "cw_bf")
    bdc = emit_l0f_bdc_image(P, cx, rm_d, rup_d)
    emit_l0_ffn(P, cx, xn2_d, bo_d, bdc, cw_bf, tag="l0f")
    emit_boundary(P, cx, r1_d, bo_d, r2_d, xn3_d, tag="b2")
    return P.materialize()


def _build_launch_C():
    nc, P, cx = _new_prog()
    xn3_d = P.dram("Xn3", (8 * 128 * 4096,), BF16, kind="ExternalInput")
    r2_d = P.dram("R2", (4096 * 1024,), F32, kind="ExternalInput")
    wds = {}
    shapes = {"Wq": (1024, 1024), "Wk": (1024, 1024), "Wv": (1024, 1024),
              "Wg1": (128, 1024), "Wg2": (1024, 128), "Wo": (1024, 1024)}
    for k, shp in shapes.items():
        wds[k] = P.dram(f"L1_attn_{k}", shp, F32, kind="ExternalInput")
    gnw_d = P.dram("L1_attn_g_norm_w", (1024,), F32, kind="ExternalInput")
    gam_d = P.dram("L1_ffn_gamma", (1024,), F32, kind="ExternalInput")
    bet_d = P.dram("L1_ffn_beta", (1024,), F32, kind="ExternalInput")
    bo_d = P.dram("BO3", (4096 * 1024,), F32)
    r3_d = P.dram("R3", (4096 * 1024,), F32, kind="ExternalOutput")
    xn4_d = P.dram("Xn4", (8 * 128 * 4096,), BF16, kind="ExternalOutput")
    st_d = P.dram("ST", (32 * 128 * 2,), F32, kind="ExternalOutput")
    cx.setup()
    wc = {k: cast_cache(P, wds[k], f"{k}_bf") for k in wds}
    ga_rep = make_rep_row(P, cx, gam_d, 0, 1024, "gam", "rg")
    be_rep = make_rep_row(P, cx, bet_d, 0, 1024, "bet", "rb")
    emit_l1_attn(P, cx, xn3_d, bo_d, wc, gnw_d, tag="l1a")
    emit_boundary(P, cx, r2_d, bo_d, r3_d, xn4_d, stats_out=st_d,
                  gamma_rep=ga_rep, beta_rep=be_rep, tag="b3")
    return P.materialize()


def _build_launch_D():
    nc, P, cx = _new_prog()
    xn4_d = P.dram("Xn4", (8 * 128 * 4096,), BF16, kind="ExternalInput")
    r3_d = P.dram("R3", (4096 * 1024,), F32, kind="ExternalInput")
    st_d = P.dram("ST", (32 * 128 * 2,), F32, kind="ExternalInput")
    wg_d = P.dram("L1_ffn_Wg", (8, 1024), F32, kind="ExternalInput")
    w1_d = P.dram("L1_ffn_W1", (8, 1024, 4096), F32, kind="ExternalInput")
    b1_d = P.dram("L1_ffn_b1", (8, 4096), F32, kind="ExternalInput")
    w2_d = P.dram("L1_ffn_W2", (8, 4096, 1024), F32, kind="ExternalInput")
    b2_d = P.dram("L1_ffn_b2", (8, 1024), F32, kind="ExternalInput")
    moe_d = P.dram("MOE", (4096 * 1024,), F32)
    out_d = P.dram("OUT", (2, 2048, 1024), F32, kind="ExternalOutput")
    cx.setup()
    emit_l1_ffn(P, cx, xn4_d, moe_d, wg_d, w1_d, b1_d, w2_d, b2_d, tag="l1f")
    emit_boundary(P, cx, r3_d, moe_d, out_d, None, prep_stats=st_d, tag="b4")
    return P.materialize()


def kernel(**inputs):
    ncores = 8
    x = np.ascontiguousarray(inputs["L0_attn_x"])
    shards = [np.ascontiguousarray(x[2 * c:2 * c + 2]) for c in range(ncores)]
    x_flat = x.reshape(16 * 2048, 1024)
    xs_shards = [np.ascontiguousarray(
        np.stack([x_flat[2 * c::16], x_flat[2 * c + 1::16]])) for c in range(ncores)]
    wk = {k: np.ascontiguousarray(v) for k, v in inputs.items() if k != "L0_attn_x"}

    ncA = _build_launch_A()
    imA = [{"L0_attn_x": shards[c], "XS": xs_shards[c],
            **{k: wk[k] for k in ["L0_attn_qkv_w", "L0_attn_qkv_b", "L0_attn_proj_w",
                                  "L0_attn_proj_b", "L0_attn_rel_emb"]}}
           for c in range(ncores)]
    rA = run_bass_kernel_spmd(ncA, imA, core_ids=list(range(ncores))).results

    ncB = _build_launch_B()
    imB = [{"Xn2": rA[c]["Xn2"], "R1": rA[c]["R1"],
            **{k: wk[k] for k in ["L0_ffn_rail_maker", "L0_ffn_rail_up_proj",
                                  "L0_ffn_w_cproj"]}} for c in range(ncores)]
    rB = run_bass_kernel_spmd(ncB, imB, core_ids=list(range(ncores))).results

    ncC = _build_launch_C()
    imC = [{"Xn3": rB[c]["Xn3"], "R2": rB[c]["R2"],
            **{k: wk[k] for k in ["L1_attn_Wq", "L1_attn_Wk", "L1_attn_Wv",
                                  "L1_attn_Wg1", "L1_attn_Wg2", "L1_attn_g_norm_w",
                                  "L1_attn_Wo", "L1_ffn_gamma", "L1_ffn_beta"]}}
           for c in range(ncores)]
    rC = run_bass_kernel_spmd(ncC, imC, core_ids=list(range(ncores))).results

    ncD = _build_launch_D()
    imD = [{"Xn4": rC[c]["Xn4"], "R3": rC[c]["R3"], "ST": rC[c]["ST"],
            **{k: wk[k] for k in ["L1_ffn_Wg", "L1_ffn_W1", "L1_ffn_b1",
                                  "L1_ffn_W2", "L1_ffn_b2"]}} for c in range(ncores)]
    rD = run_bass_kernel_spmd(ncD, imD, core_ids=list(range(ncores))).results

    out = np.concatenate([rD[c]["OUT"].reshape(2, 2048, 1024) for c in range(ncores)],
                         axis=0)
    return out
